# version 15
# baseline (speedup 1.0000x reference)
import numpy as np
import concourse.bass as bass
import concourse.mybir as mybir
from concourse.bass_utils import run_bass_kernel_spmd

F32 = mybir.dt.float32
BF16 = mybir.dt.bfloat16
I32 = mybir.dt.int32
U8 = mybir.dt.uint8
AF = mybir.ActivationFunctionType
ALU = mybir.AluOpType

E = 1024
NH = 16
HD = 64
NB = 8
MT = 512
CH = 128
NE = 32
EI = 512
EPS = 1e-5



class Buf:
    __slots__ = ("w", "r", "name")

    def __init__(self, name=""):
        self.w = []
        self.r = []
        self.name = name


class MK:
    ENGS = ("t", "v", "s", "g", "y")

    SERIAL = False

    def __init__(self, nc):
        self.nc = nc
        self.eng = {"t": nc.tensor, "v": nc.vector, "s": nc.scalar,
                    "g": nc.gpsimd, "y": nc.sync}
        self.sem = {k: nc.alloc_semaphore(name=f"ms_{k}") for k in self.ENGS}
        self.cnt = {k: 0 for k in self.ENGS}
        self.steps = {k: [] for k in self.ENGS}
        self.dsem = {}

    def dma(self, name):
        if name not in self.dsem:
            self.dsem[name] = [self.nc.alloc_semaphore(name=f"dma_{name}"), 0]
        return self.dsem[name]

    def step(self, ek, fns, reads=(), writes=(), dma=None, tag="", no_selfwait=False):
        if not isinstance(fns, (list, tuple)):
            fns = [fns]
        deps = {}
        selfw = [0]

        def add(tok):
            sem, val = tok
            if sem is self.sem[ek]:
                if val >= self.cnt[ek] - 1:
                    selfw[0] = max(selfw[0], val)
                return
            k = sem.num
            if k not in deps or deps[k][1] < val:
                deps[k] = (sem, val)

        for b in reads:
            for t in b.w:
                add(t)
        for b in writes:
            for t in b.w:
                add(t)
            for t in b.r:
                add(t)
        if self.SERIAL:
            for k2 in self.ENGS:
                if self.cnt[k2]:
                    add((self.sem[k2], self.cnt[k2]))
            for (h, c) in self.dsem.values():
                if c:
                    add((h, c))
        if selfw[0] and not no_selfwait:
            deps[self.sem[ek].num] = (self.sem[ek], selfw[0])
        if dma is None:
            self.cnt[ek] += 1
            tok = (self.sem[ek], self.cnt[ek])
            inc = (self.sem[ek], 1)
        else:
            ds = self.dma(dma)
            ds[1] += 16
            tok = (ds[0], ds[1])
            inc = (ds[0], 16)
        self.steps[ek].append((list(deps.values()), fns, inc, tag))
        for b in reads:
            b.r.append(tok)
        for b in writes:
            b.w = [tok]
            b.r = []
        return tok

    def barrier(self, skip=()):
        toks = [(self.sem[k], self.cnt[k]) for k in self.ENGS]
        toks += [(h, c) for (h, c) in self.dsem.values()]
        for k in self.ENGS:
            if k in skip:
                continue
            deps = [(s, v) for (s, v) in toks if s is not self.sem[k]]
            self.cnt[k] += 1
            self.steps[k].append(
                (deps, [lambda e: e.nop()], (self.sem[k], 1), "barrier"))

    def finalize(self):
        self.barrier()

    def emit(self):
        with self.nc.Block() as block:
            for k in self.ENGS:
                def runner(e, k=k):
                    seen = {}
                    for deps, fns, inc, tag in self.steps[k]:
                        for sem, val in deps:
                            if seen.get(sem.num, -1) < val:
                                e.wait_ge(sem, val)
                                seen[sem.num] = val
                        inst = None
                        for f in fns:
                            inst = f(e)
                        inst.then_inc(inc[0], inc[1])
                getattr(block, {"t": "tensor", "v": "vector", "s": "scalar",
                                "g": "gpsimd", "y": "sync"}[k])(runner)



def build(T=4096, SEG=1024, dbg=()):
    assert T % SEG == 0 and SEG % MT == 0
    NSEG = T // SEG
    NMT = SEG // MT

    nc = bass.Bass("TRN2", target_bir_lowering=False)
    mk = MK(nc)

    dt_in = {}

    def din(name, shape, dtype=F32):
        h = nc.dram_tensor(name, list(shape), dtype, kind="ExternalInput").ap()
        dt_in[name] = h
        return h

    x_h = din("x", (T, E))
    Wq_h = din("Wq", (E, E)); Wk_h = din("Wk", (E, E)); Wv_h = din("Wv", (E, E))
    Wo_h = din("Wo", (E, E))
    dlt_h = din("delta", (E,))
    G1_h = din("G1", (E, HD)); G2_h = din("G2", (HD, E))
    nw_h = din("norm_w", (E,))
    gwT_h = din("gwT", (E, NE))
    gb_h = din("gate_bias", (NE,))
    wgT_h = din("wgT", (NE, E, EI)); wuT_h = din("wuT", (NE, E, EI))
    wdT_h = din("wdT", (NE, EI, E))
    shgT_h = din("shgT", (E, E)); shuT_h = din("shuT", (E, E))
    shdT_h = din("shdT", (E, E))

    out_h = nc.dram_tensor("out", [T, E], F32, kind="ExternalOutput").ap()
    on_h = nc.dram_tensor("on_scr", [NB, 128, T], BF16, kind="Internal").ap()

    P = [nc.alloc_psum_tensor(f"bank{i}", [128, 512], F32) for i in range(8)]
    PB = [Buf(f"P{i}") for i in range(8)]
    P6b = P[6].ap().bitcast(BF16)
    P7b = P[7].ap().bitcast(BF16)

    def sb(name, shape, dtype):
        return nc.alloc_sbuf_tensor(name, list(shape), dtype)

    xnH = sb("xnH", (128, NB, SEG), BF16)
    cF = sb("cF", (NE, SEG), BF16)
    SA = sb("SA", (128, NB, 128), F32)
    SBf = sb("SBf", (128, NB, 128), BF16)
    kT = sb("kT", (128, 2, E), BF16)
    Abar = sb("Abar", (128, 2, NH, 128), BF16)
    ECb = sb("ECb", (128, NB, 4), F32)
    tri = sb("tri", (128, 128), F32)
    ident = sb("ident", (128, 128), BF16)
    ones1 = sb("ones1", (1, 128), BF16)
    ones1f = sb("ones1f", (1, 128), F32)
    sel32 = sb("sel32", (NE, NE, 128), BF16)
    id32 = sb("id32", (NE, NE), BF16)
    id16 = sb("id16", (16, 16), BF16)
    hi2 = sb("hi2", (128, 2), BF16)
    sel16 = sb("sel16", (128, NB, 16), BF16)
    selB = sb("selB", (16, NB, 128), BF16)
    posd = sb("posd", (128, NB), F32)
    negd = sb("negd", (128, NB), F32)
    nwF = sb("nwF", (128, NB), F32)
    gwF = sb("gwF", (128, NB, NE), F32)
    gwH = sb("gwH", (128, NB, NE), BF16)
    gwL = sb("gwL", (128, NB, NE), BF16)
    gb1 = sb("gb1", (1, NE), F32)
    gbB = sb("gbB", (128, NE), F32)
    epsH = sb("epsH", (16, 1), F32)
    epsLN = sb("epsLN", (128, 1), F32)
    G1b = sb("G1b", (128, NB, HD), BF16)
    G2b = sb("G2b", (HD, E), BF16)
    rsH = sb("rsH", (16, 512), BF16)
    rsL = sb("rsL", (16, 512), F32)
    bnst = sb("bnst", (128, 12), F32)
    bnag = sb("bnag", (128, 2), F32)
    lnt = sb("lnt", (128, 1), F32)
    scr = sb("scr", (128, 2, 4, NE), F32)
    scb = sb("scb", (128, 2, 4, NE), F32)
    gsum = sb("gsum", (128, 2, 4, 8), F32)
    gmax = sb("gmax", (128, 2, 4, 8), F32)
    gzap = sb("gzap", (128, 2, 4, 8), F32)
    emax = sb("emax", (128, 2, 4, 8), F32)
    ezap = sb("ezap", (128, 2, 4, NE), F32)
    selm = sb("selm", (128, 2, 4, NE), F32)
    tkw = sb("tkw", (128, 2, 4, NE), F32)
    tsum = sb("tsum", (128, 2, 4), F32)
    rt1 = sb("rt1", (128, 2, 4, 8), F32)
    rt2 = sb("rt2", (128, 2, 4, 8), F32)
    rt3 = sb("rt3", (128, 2, 4, 8), F32)
    combT = sb("combT", (128, 2, 128), BF16)

    ARENA = 147456
    arena = nc.alloc_sbuf_tensor("arena", [128, ARENA], U8)
    arena_addr = nc.lookup_mloc(arena).addr

    def place(name, shape, dtype, off):
        per = int(np.prod(shape[1:])) * mybir.dt.size(dtype)
        assert off + per <= ARENA, (name, off, per)
        h = nc.virtual_sbuf_tensor(name, list(shape), dtype)
        nc._tensor(h.name, h.shape, h.dtype, type="SB")
        mloc = nc.lookup_mloc(h)
        mloc.allocated = True
        mloc.addr = arena_addr + off
        return h

    xA = place("xA", (128, 2, 1024), F32, 0)
    WkB = place("WkB", (128, NB, E), BF16, 8192)
    tmA = place("tmA", (128, 1024), BF16, 24576)
    tmB = place("tmB", (128, 1024), BF16, 26624)
    xg1 = place("xg1", (HD, 512), BF16, 28672)
    WvB = place("WvB", (128, NB, E), BF16, 32768)
    WqB = place("WqB", (128, NB, E), BF16, 49152)
    fg = place("fg", (128, NB, 512), F32, 65536)
    qsil = place("qsil", (128, NB, 512), BF16, 81920)
    kk = place("kk", (128, NB, 512), BF16, 90112)
    vv = place("vv", (128, 4, 1024), BF16, 98304)
    gsb = place("gsb", (128, NB, 512), BF16, 106496)
    eq = place("eq", (128, NB, 512), BF16, 114688)
    of = place("of", (128, NB, 512), BF16, 122880)
    xnL = place("xnL", (128, NB, 512), BF16, 131072)
    ek = place("ek", (128, NB, 512), BF16, 131072)
    onT0 = place("onT0", (128, NB, 512), BF16, 0)
    WoS = place("WoS", (128, NB, E), BF16, 8192)
    onT1 = place("onT1", (128, NB, 512), BF16, 24576)
    wgB1 = place("wgB1", (128, NB, 512), BF16, 32768)
    wuB1 = place("wuB1", (128, NB, 512), BF16, 40960)
    mE0 = place("mE0", (128, 2, 4, 512), BF16, 49152)
    wgB0 = place("wgB0", (128, NB, 512), BF16, 57344)
    mE1 = place("mE1", (128, 2, 4, 512), BF16, 65536)
    wuB0 = place("wuB0", (128, NB, 512), BF16, 73728)
    wdB0 = place("wdB0", (128, 4, 1024), BF16, 81920)
    wdB1 = place("wdB1", (128, 4, 1024), BF16, 90112)
    sgS = place("sgS", (128, 2, 512), BF16, 98304)
    bcS = place("bcS", (128, 2, 512), BF16, 100352)
    outS = place("outS", (128, 2, 4, 1024), F32, 114688)
    iotaI = place("iotaI", (128, 128), I32, 139264)
    wgB = [wgB0, wgB1]; wuB = [wuB0, wuB1]; wdB = [wdB0, wdB1]
    onTs = [onT0, onT1]; mEt = [mE0, mE1]

    rem = nc.sbuf_bytes_remaining
    assert rem >= 0, f"SBUF over budget by {-rem} bytes"

    _dumped = set()

    def dump(tag, src_ap, read_bufs):
        if tag not in dbg or tag in _dumped:
            return
        _dumped.add(tag)
        h = nc.dram_tensor("dbg_" + tag, list(src_ap.shape), src_ap.dtype,
                           kind="ExternalOutput").ap()
        idx = tuple(slice(None) for _ in src_ap.shape)
        mk.step("y", lambda e: e.dma_start(out=h[idx], in_=src_ap),
                reads=list(read_bufs), dma="dbg_%s" % tag, tag="dbg")

    B_ = {}

    def buf(name):
        if name not in B_:
            B_[name] = Buf(name)
        return B_[name]

    st = mk.step
    st("g", lambda e: e.iota(iotaI[:], pattern=[[-1, 128]], base=0,
                             channel_multiplier=1), writes=[buf("iota")])
    st("v", lambda e: e.tensor_scalar(tri[:], iotaI[:], 0.0, None, ALU.is_le),
       reads=[buf("iota")], writes=[buf("tri")])
    st("v", lambda e: e.tensor_scalar(ident[:], iotaI[:], 0.0, None, ALU.is_equal),
       reads=[buf("iota")], writes=[buf("ident")])
    st("v", lambda e: e.memset(ones1[:], 1.0), writes=[buf("ones1")])
    st("v", lambda e: e.memset(ones1f[:], 1.0), writes=[buf("ones1f")])
    st("v", lambda e: e.memset(epsH[:], EPS), writes=[buf("epsH")])
    st("v", lambda e: e.memset(epsLN[:], EPS), writes=[buf("epsLN")])
    st("v", lambda e: e.memset(SA[:], 0.0), writes=[buf("SA")])
    st("v", lambda e: e.memset(SBf[:], 0.0), writes=[buf("SBf")])
    st("v", lambda e: e.memset(sel16[:], 0.0), writes=[buf("sel16")])
    st("v", lambda e: e.tensor_scalar(id32[:], iotaI[0:NE, 0:NE], 0.0, None, ALU.is_equal),
       reads=[buf("iota")], writes=[buf("id32")])
    st("v", lambda e: e.tensor_copy(
        sel32[:], id32[:].unsqueeze(2).broadcast_to((NE, NE, 128))),
       reads=[buf("id32")], writes=[buf("sel32")])
    st("v", lambda e: e.tensor_scalar(id16[:], iotaI[0:16, 0:16], 0.0, None, ALU.is_equal),
       reads=[buf("iota")], writes=[buf("id16")])
    st("v", lambda e: e.tensor_copy(
        selB[:].rearrange("k b (h r) -> k b h r", h=2),
        id16[:].rearrange("k (b h) -> k b h", h=2).unsqueeze(3).broadcast_to((16, NB, 2, 64))),
       reads=[buf("id16")], writes=[buf("selB")])
    st("v", lambda e: e.tensor_scalar(hi2[:, 0:1], iotaI[:, 0:1], 64.0, None, ALU.is_lt),
       reads=[buf("iota")], writes=[buf("hi2")])
    st("v", lambda e: e.tensor_scalar(hi2[:, 1:2], iotaI[:, 0:1], 63.0, None, ALU.is_gt),
       reads=[buf("iota")], writes=[buf("hi2")])
    sel16f = sel16[:].rearrange("p a b -> p (a b)")
    for h2 in range(2):
        import dataclasses as _dc
        dst = _dc.replace(sel16f, offset=sel16f.offset + h2,
                          ap=[list(sel16f.ap)[0], [18, 8]])
        st("v", lambda e, dst=dst, h2=h2: e.tensor_copy(
            dst, hi2[:, h2:h2 + 1].broadcast_to((128, 8))),
           reads=[buf("hi2")], writes=[buf("sel16")])
    st("y", lambda e: e.dma_start(out=posd[:], in_=dlt_h.rearrange("(b p) -> p b", p=128), allow_slow_non_contiguous=True),
       writes=[buf("posd")], dma="ldposd")
    st("v", lambda e: e.tensor_scalar(negd[:], posd[:], -1.0, None, ALU.mult),
       reads=[buf("posd")], writes=[buf("negd")])
    st("y", lambda e: e.dma_start(out=nwF[:], in_=nw_h.rearrange("(b p) -> p b", p=128), allow_slow_non_contiguous=True),
       writes=[buf("nwF")], dma="ldnw")
    st("y", lambda e: e.dma_start(out=gwF[:], in_=gwT_h.rearrange("(b p) ne -> p b ne", p=128)),
       writes=[buf("gwF")], dma="ldgw")
    st("v", lambda e: e.tensor_copy(gwH[:], gwF[:]), reads=[buf("gwF")], writes=[buf("gwH")])
    st("v", lambda e: e.scalar_tensor_tensor(gwL[:], gwF[:], 0.0, gwH[:],
                                             ALU.bypass, ALU.subtract),
       reads=[buf("gwF"), buf("gwH")], writes=[buf("gwL")])
    st("y", lambda e: e.dma_start(out=gb1[:], in_=gb_h[None, :]),
       writes=[buf("gb1")], dma="ldgb")
    st("t", lambda e: e.matmul(P[0].ap()[:, 0:NE], ones1f[0:1, :], gb1[0:1, :],
                               start=True, stop=True),
       reads=[buf("ones1f"), buf("gb1")], writes=[PB[0]])
    st("v", lambda e: e.tensor_copy(gbB[:], P[0].ap()[:, 0:NE]),
       reads=[PB[0]], writes=[buf("gbB")])
    st("g", lambda e: e.dma_start(out=G1b[:], in_=G1_h.rearrange("(b p) d -> p b d", p=128)),
       writes=[buf("G1b")], dma="ldg1")
    st("g", lambda e: e.dma_start(out=G2b[:], in_=G2_h[:, :]),
       writes=[buf("G2b")], dma="ldg2")

    bxA = [buf("xA0"), buf("xA1")]
    bfg = buf("fg"); bqs = buf("qsil"); bkk = buf("kk"); bvv = buf("vv")
    bgs = buf("gsb"); beq = buf("eq"); bof = buf("of")
    bxnL = buf("xnL"); bek = bxnL; btmA = buf("tmA"); btmB = buf("tmB"); bxg1 = buf("xg1")
    bWq = buf("WqB"); bWk = buf("WkB"); bWv = buf("WvB")
    bxnH = buf("xnH"); bcFm = [buf("cF0"), buf("cF1")]; bSA = buf("SA"); bSBf = buf("SBf")
    bkTs = [buf("kT0"), buf("kT1")]
    bAbw = [[buf("Abar00"), buf("Abar01")], [buf("Abar10"), buf("Abar11")]]
    bEC = buf("ECb")
    brt = buf("rt")
    brsH = buf("rsH"); brsL = buf("rsL"); bbn = buf("bn")
    bon = buf("on_scr")
    bmE = [[buf("mE%d%d" % (a, b)) for b in range(2)] for a in range(2)]
    bwg = [buf("wg0"), buf("wg1")]
    bwu = [buf("wu0"), buf("wu1")]; bwd = [buf("wd0"), buf("wd1")]
    bWo = buf("WoS"); bonT = [buf("onT0"), buf("onT1")]
    boutS = [buf("outS0"), buf("outS1")]
    bsg = [buf("sg0"), buf("sg1")]; bbc = [buf("bc0"), buf("bc1")]
    bout = buf("out_h")

    AL_onT = [[bxA[0], bxA[1]], [btmA, btmB, bxg1]]
    AL_wg = [[bWq], [bWv]]
    AL_wu = [[bfg], [bWv]]
    AL_wd = [[bqs], [bkk]]
    AL_Wo = [bWk]
    AL_mE = [[bWq], [bfg]]
    AL_outS = [beq, bof, bxnL, buf("iota")]
    AL_sg = [bvv]
    AL_bc = [bvv]
    AL_xA = [bonT[0]]
    AL_fg = bmE[1] + [bwu[0]]
    AL_qsil = [bwd[0]]
    AL_kk = [bwd[1]]
    AL_vv = [bsg[0], bsg[1], bbc[0], bbc[1]]
    AL_gseq = [boutS[0], boutS[1]]
    AL_ofxnL = [boutS[0], boutS[1]]
    AL_tmA = [bonT[1]]
    AL_tmB = [bonT[1]]
    AL_WqB = bmE[0] + [bwg[0]]
    AL_WkB = [bWo]
    AL_WvB = [bwg[1], bwu[1]]

    def phase_a(seg):
        s0 = seg * SEG
        pend = []
        def emit_rms(m, t0, toff):
                st("t", [lambda e, blk=blk: e.matmul(
                    P[0].ap()[0:16, :], sel16[:, blk, :], eq[:, blk, :],
                    start=(blk == 0), stop=(blk == 7)) for blk in range(NB)],
                   reads=[beq, buf("sel16")], writes=[PB[0]], tag="rmssum")
                st("s", lambda e: e.activation(rsL[:], P[0].ap()[0:16, :], AF.Ln,
                                               bias=epsH[:], scale=1.0 / HD),
                   reads=[PB[0], buf("epsH")], writes=[brsL], tag="rms_rs")
                st("s", lambda e: e.activation(rsH[:], rsL[:], AF.Exp, scale=-0.5),
                   reads=[brsL], writes=[brsH], tag="rms_rs")
                for blk in range(NB):
                    pbk = 2 + blk % 2
                    st("t", lambda e, blk=blk, pbk=pbk: e.matmul(
                        P[pbk].ap()[:, :], selB[:, blk, :], rsH[:],
                        start=True, stop=True),
                       reads=[brsH, buf("selB")], writes=[PB[pbk]], tag="rsbc")
                    st("v", lambda e, blk=blk, pbk=pbk: e.tensor_tensor(
                        of[:, blk, :], of[:, blk, :], P[pbk].ap()[:, :], ALU.mult),
                       reads=[PB[pbk], bof], writes=[bof] + AL_ofxnL, tag="onmul")
                dump("A_of", of[:, :, :], [bof])
                dump("A_SA", SA[:, :, :], [bSA])
                dump("A_Abar", Abar[:, 1, :, :], [bAbw[1][0], bAbw[1][1]])
                ondst = on_h[:, :, t0:t0 + MT].transpose([1, 0, 2])
                st("y", lambda e, ondst=ondst: e.dma_start(out=ondst, in_=of[:]),
                   reads=[bof], writes=[bon], dma="onst", tag="onstore")

        for nm, W_h, bW, dst in (("wqs", Wq_h, bWq, WqB), ("wks", Wk_h, bWk, WkB),
                                 ("wvs", Wv_h, bWv, WvB)):
            st("g", lambda e, W_h=W_h, dst=dst: e.dma_start(
                out=dst[:], in_=W_h.rearrange("(kc p) oc -> p kc oc", p=128)),
               writes=[bW] + (AL_WqB if dst is WqB else AL_WkB if dst is WkB else AL_WvB), dma=nm, tag="wqkv")

        for m in range(NMT):
            t0 = s0 + m * MT
            toff = m * MT
            for g in range(4):
                slot = g % 2
                xa = xA[:, slot, :]
                xsrc = x_h[t0 + 128 * g: t0 + 128 * (g + 1), :]
                st("y", lambda e, xa=xa, xsrc=xsrc: e.dma_start(out=xa, in_=xsrc),
                   writes=[bxA[slot]] + AL_xA, dma="xin%d" % slot, tag="xload")
                st("v", [lambda e, xa=xa: e.bn_stats(bnst[:, 0:6], xa[:, 0:512]),
                         lambda e, xa=xa: e.bn_stats(bnst[:, 6:12], xa[:, 512:1024])],
                   reads=[bxA[slot]], writes=[bbn])
                st("v", lambda e: e.bn_aggr(bnag[:], bnst[:]),
                   reads=[bbn], writes=[bbn])
                st("s", lambda e: e.activation(lnt[:], bnag[:, 1:2], AF.Ln, bias=epsLN[:]),
                   reads=[bbn, buf("epsLN")], writes=[buf("lnt")])
                st("s", lambda e: e.activation(lnt[:], lnt[:], AF.Exp, scale=-0.5),
                   reads=[buf("lnt")], writes=[buf("lnt")])
                st("v", lambda e, xa=xa: e.tensor_scalar(
                    xa, xa, bnag[:, 0:1], lnt[:], ALU.subtract, ALU.mult),
                   reads=[bbn, buf("lnt")], writes=[bxA[slot]] + AL_xA)
                st("s", lambda e, xa=xa: e.copy(tmA[:], xa),
                   reads=[bxA[slot]], writes=[btmA] + AL_tmA)
                st("v", lambda e, xa=xa: e.scalar_tensor_tensor(
                    tmB[:], xa, 0.0, tmA[:], ALU.bypass, ALU.subtract),
                   reads=[bxA[slot], btmA], writes=[btmB] + AL_tmB)
                st("t", [lambda e, b=b: e.transpose(
                    P6b[:, 128 * b:128 * (b + 1)], tmA[:, 128 * b:128 * (b + 1)], ident[:])
                    for b in range(NB)],
                   reads=[btmA, buf("ident")], writes=[PB[6]], tag="xntr")
                st("t", [lambda e, b=b: e.transpose(
                    P7b[:, 128 * b:128 * (b + 1)], tmB[:, 128 * b:128 * (b + 1)], ident[:])
                    for b in range(NB)],
                   reads=[btmB, buf("ident")], writes=[PB[7]], tag="xntr")
                xnHdst = xnH[:, :, toff + 128 * g: toff + 128 * (g + 1)]
                st("s", lambda e, xnHdst=xnHdst: e.copy(
                    xnHdst, P6b[:].rearrange("p (b t) -> p b t", b=NB)),
                   reads=[PB[6]], writes=[bxnH])
                st("s", lambda e, g=g: e.copy(
                    xnL[:, :, 128 * g: 128 * (g + 1)],
                    P7b[:].rearrange("p (b t) -> p b t", b=NB)),
                   reads=[PB[7]], writes=[bxnL] + AL_ofxnL)

            xnHm = xnH[:, :, toff:toff + MT]
            def rfns():
                fs = []
                for g in range(4):
                    out = P[2].ap()[:, NE * g:NE * (g + 1)]
                    for i, (lh, rh) in enumerate(
                            ((xnHm, gwH), (xnHm, gwL), (xnL, gwH))):
                        for kc in range(NB):
                            lhs = lh[:, kc, 128 * g:128 * (g + 1)]
                            fs.append(lambda e, out=out, lhs=lhs, rh=rh, kc=kc,
                                      first=(i == 0 and kc == 0),
                                      last=(i == 2 and kc == 7): e.matmul(
                                out, lhs, rh[:, kc, :], start=first, stop=last))
                return fs
            st("t", rfns(), reads=[bxnH, bxnL, buf("gwH"), buf("gwL")],
               writes=[PB[2]], tag="router_mm")
            scrdst = scr[:, m, :, :].rearrange("p a b -> p (a b)")
            st("s", lambda e, scrdst=scrdst: e.activation(
                scrdst, P[2].ap()[:, 0:128], AF.Sigmoid),
               reads=[PB[2]], writes=[brt], tag="sig")
            while pend and pend[0][0] < m:
                emit_rms(*pend.pop(0))
            xnHm = xnH[:, :, toff:toff + MT]

            def proj_fm(dstW, bW, pb, blk, extra_reads):
                def fns(xnHm=xnHm):
                    out = P[pb].ap()[:, :]
                    return [lambda e, kc=kc, out=out, xnHm=xnHm: e.matmul(
                        out, dstW[:, kc, 128 * blk:128 * (blk + 1)],
                        xnHm[:, kc, :], start=(kc == 0), stop=(kc == 7))
                        for kc in range(NB)]
                return st("t", fns(), reads=[bW, bxnH] + extra_reads,
                          writes=[PB[pb]], tag="proj")

            for blk in range(NB):
                pb = blk % 4
                proj_fm(WkB, bWk, pb, blk, [])
                st("s", lambda e, pb=pb, blk=blk: e.activation(
                    kk[:, blk, :], P[pb].ap()[:, :], AF.Sigmoid,
                    bias=negd[:, blk:blk + 1], scale=-1.0),
                   reads=[PB[pb], buf("negd")], writes=[bkk] + AL_kk, tag="sig")
                st("s", lambda e, pb=pb, blk=blk: e.activation(
                    fg[:, blk, :], P[pb].ap()[:, :], AF.Sigmoid,
                    bias=posd[:, blk:blk + 1], scale=1.0),
                   reads=[PB[pb], buf("posd")], writes=[bfg] + AL_fg, tag="sig")
            st("t", [lambda e, kc=kc, xnHm=xnHm: e.matmul(
                P[0].ap()[0:HD, :], G1b[:, kc, :], xnHm[:, kc, :],
                start=(kc == 0), stop=(kc == 7)) for kc in range(NB)],
               reads=[buf("G1b"), bxnH], writes=[PB[0]], tag="xg1")
            st("v", lambda e: e.tensor_copy(xg1[:], P[0].ap()[0:HD, :]),
               reads=[PB[0]], writes=[bxg1, bonT[1]])
            for blk in range(NB):
                pb = blk % 4
                st("t", lambda e, blk=blk, pb=pb: e.matmul(
                    P[pb].ap()[:, :], G2b[:, 128 * blk:128 * (blk + 1)], xg1[:],
                    start=True, stop=True),
                   reads=[buf("G2b"), bxg1], writes=[PB[pb]], tag="gate")
                st("s", lambda e, pb=pb, blk=blk: e.activation(
                    gsb[:, blk, :], P[pb].ap()[:, :], AF.Sigmoid),
                   reads=[PB[pb]], writes=[bgs], tag="sig")
            st("s", lambda e: e.activation(
                fg[:].rearrange("p a b -> p (a b)"),
                fg[:].rearrange("p a b -> p (a b)"), AF.Ln),
               reads=[bfg], writes=[bfg] + AL_fg, tag="ln_g")
            for blk in range(NB):
                for c in range(4):
                    sl = fg[:, blk, 128 * c:128 * (c + 1)]
                    st("v", lambda e, sl=sl: e.tensor_tensor_scan(
                        sl, sl, sl, 0.0, ALU.add, ALU.bypass),
                       reads=[bfg], writes=[bfg] + AL_fg, no_selfwait=True)
            st("s", lambda e: e.activation(
                ECb[:].rearrange("p a b -> p (a b)"),
                fg[:, :, 127::128].rearrange("p a b -> p (a b)"), AF.Exp),
               reads=[bfg], writes=[bEC], tag="exp")
            st("s", lambda e: e.activation(
                eq[:].rearrange("p a b -> p (a b)"),
                fg[:].rearrange("p a b -> p (a b)"), AF.Exp, scale=1.0),
               reads=[bfg], writes=[beq] + AL_gseq, tag="exp")
            st("s", lambda e: e.activation(
                ek[:].rearrange("p a b -> p (a b)"),
                fg[:].rearrange("p a b -> p (a b)"), AF.Exp, scale=-1.0),
               reads=[bfg], writes=[bek] + AL_ofxnL, tag="exp")
            st("v", lambda e: e.tensor_tensor(
                kk[:].rearrange("p a b -> p (a b)"),
                kk[:].rearrange("p a b -> p (a b)"),
                ek[:].rearrange("p a b -> p (a b)"), ALU.mult),
               reads=[bkk, bek], writes=[bkk] + AL_kk)
            for blk in range(NB):
                pb = blk % 4
                proj_fm(WqB, bWq, pb, blk, [])
                st("s", lambda e, pb=pb, blk=blk: e.activation(
                    qsil[:, blk, :], P[pb].ap()[:, :], AF.Silu),
                   reads=[PB[pb]], writes=[bqs] + AL_qsil, tag="qsilu")
            st("v", lambda e: e.tensor_tensor(
                qsil[:].rearrange("p a b -> p (a b)"),
                qsil[:].rearrange("p a b -> p (a b)"),
                eq[:].rearrange("p a b -> p (a b)"), ALU.mult),
               reads=[bqs, beq], writes=[bqs] + AL_qsil)


            dump("A_xn", xnH[:, :, toff:toff + MT], [bxnH])
            dump("A_bnag", bnag[:, :], [bbn])
            dump("A_lnt", lnt[:, :], [buf("lnt")])
            dump("A_xa", xA[:, :, :], [bxA[0], bxA[1]])
            dump("A_xnL", xnL[:, :, :], [bxnL])
            dump("A_B", fg[:, :, :], [bfg])
            dump("A_q", qsil[:, :, :], [bqs])
            dump("A_k", kk[:, :, :], [bkk])
            dump("A_v", vv[:, :, :], [bvv])
            dump("A_gs", gsb[:, :, :], [bgs])
            dump("A_eq", eq[:, :, :], [beq])
            dump("A_EC", ECb[:, :, :], [bEC])
            dump("A_cF", cF[:, 0:MT], [bcFm[0]])
            for g in range(4):
                for oh in range(2):
                    pb = (g * 2 + oh) % 4

                    def fns(g=g, oh=oh, pb=pb, xnHm=xnHm):
                        out = P[pb].ap()[:, :]
                        return [lambda e, kc=kc, g=g, oh=oh, out=out, xnHm=xnHm:
                                e.matmul(
                            out, xnHm[:, kc, 128 * g:128 * (g + 1)],
                            WvB[:, kc, 512 * oh:512 * (oh + 1)],
                            start=(kc == 0), stop=(kc == 7)) for kc in range(NB)]
                    st("t", fns(), reads=[bWv, bxnH], writes=[PB[pb]], tag="vproj")
                    st("v", lambda e, pb=pb, g=g, oh=oh: e.tensor_copy(
                        vv[:, g, 512 * oh:512 * (oh + 1)], P[pb].ap()[:, :]),
                       reads=[PB[pb]], writes=[bvv] + AL_vv)
            def emit_ktr(c):
                tc = 128 * c
                st("t", [lambda e, b=b, src=kk[:, b, tc:tc + 128]: e.transpose(
                    P6b[:, 128 * b:128 * (b + 1)], src, ident[:])
                    for b in range(NB)],
                   reads=[bkk, buf("ident")], writes=[PB[6]], tag="ktr")
                st("s", lambda e, sl=kT[:, c % 2, :]: e.copy(sl, P6b[:]),
                   reads=[PB[6]], writes=[bkTs[c % 2]], tag="ktev")

            def emit_scores(c):
                tc = 128 * c
                par = c % 2
                for w in range(2):
                    pb0 = 2 - 2 * w
                    def sfns(w=w, pb0=pb0, tc=tc):
                        fs = []
                        for j in range(4 * w, 4 * w + 4):
                            col = 128 * (j % 4)
                            for hh2 in range(2):
                                pbk = pb0 + hh2
                                lhs = kk[64 * hh2:64 * (hh2 + 1), j, tc:tc + 128]
                                rhs = qsil[64 * hh2:64 * (hh2 + 1), j, tc:tc + 128]
                                fs.append(lambda e, pbk=pbk, col=col, lhs=lhs, rhs=rhs:
                                          e.matmul(P[pbk].ap()[:, col:col + 128],
                                                   lhs, rhs, start=True, stop=True))
                        return fs
                    st("t", sfns(), reads=[bkk, bqs], writes=[PB[pb0], PB[pb0 + 1]],
                       tag="score")
                    for hh2 in range(2):
                        pbk = pb0 + hh2
                        hsel = Abar[:, par, 8 * w + hh2: 8 * w + 8: 2, :]
                        st("v", lambda e, pbk=pbk, hsel=hsel: e.tensor_tensor(
                            hsel,
                            P[pbk].ap()[:].rearrange("p (a t) -> p a t", a=4),
                            tri[:].unsqueeze(1).broadcast_to((128, 4, 128)),
                            ALU.mult),
                           reads=[PB[pbk], buf("tri")], writes=[bAbw[par][w]], tag="mask")

            def emit_tail(c):
                tc = 128 * c
                par = c % 2
                for w in range(2):
                    def ofns(c=c, tc=tc, w=w, par=par):
                        fs = []
                        for j in range(4 * w, 4 * w + 4):
                            pbk = 4 + w
                            col = 128 * (j % 4)
                            out = P[pbk].ap()[:, col:col + 128]
                            lh_i = SBf[:, j, :]
                            rh_i = qsil[:, j, tc:tc + 128]
                            fs.append(lambda e, out=out, lh_i=lh_i, rh_i=rh_i: e.matmul(
                                out, lh_i, rh_i, start=True, stop=False))
                            for hh2 in range(2):
                                h = 2 * j + hh2
                                outh = P[pbk].ap()[64 * hh2:64 * (hh2 + 1), col:col + 128]
                                vsl = vv[:, c, 128 * j + 64 * hh2:128 * j + 64 * (hh2 + 1)]
                                asl = Abar[:, par, h, :]
                                fs.append(lambda e, outh=outh, vsl=vsl, asl=asl,
                                          hh2=hh2: e.matmul(
                                    outh, vsl, asl, start=False, stop=(hh2 == 1)))
                        return fs
                    st("t", ofns(), reads=[bSBf, bqs, bvv, bAbw[par][w]],
                       writes=[PB[4 + w]], tag="omm")
                def kvfns(c=c):
                    fs = []
                    for h in range(NH):
                        r0 = 64 * (h % 2)
                        col = 64 * (h // 2)
                        out = P[7].ap()[r0:r0 + 64, col:col + 64]
                        ksl = kT[:, c % 2, 64 * h:64 * (h + 1)]
                        vsl = vv[:, c, 64 * h:64 * (h + 1)]
                        fs.append(lambda e, out=out, ksl=ksl, vsl=vsl: e.matmul(
                            out, ksl, vsl, start=True, stop=True))
                    return fs
                st("t", kvfns(), reads=[bkTs[c % 2], bvv], writes=[PB[7]], tag="kv")
                for h2 in range(2):
                    sa = SA[64 * h2:64 * (h2 + 1), :, 64 * h2:64 * (h2 + 1)]
                    ps = P[7].ap()[64 * h2:64 * (h2 + 1), :].rearrange(
                        "p (j v) -> p j v", v=64)
                    st("v", lambda e, sa=sa, ps=ps: e.tensor_tensor(sa, sa, ps, ALU.add),
                       reads=[PB[7], bSA], writes=[bSA], tag="supd")
                st("v", lambda e, c=c: e.tensor_tensor(
                    SA[:], SA[:], ECb[:, :, c:c + 1].broadcast_to((128, NB, 128)),
                    ALU.mult),
                   reads=[bSA, bEC], writes=[bSA], tag="sscale")
                st("s", lambda e: e.copy(
                    SBf[:].rearrange("p a b -> p (a b)"),
                    SA[:].rearrange("p a b -> p (a b)")),
                   reads=[bSA], writes=[bSBf], tag="sbf")
                for q in range(2):
                    pbk = 4 + q
                    st("v", lambda e, pbk=pbk, q=q, tc=tc: e.tensor_tensor(
                        of[:, 4 * q:4 * (q + 1), tc:tc + 128],
                        P[pbk].ap()[:].rearrange("p (a t) -> p a t", a=4),
                        gsb[:, 4 * q:4 * (q + 1), tc:tc + 128], ALU.mult),
                       reads=[PB[pbk], bgs], writes=[bof] + AL_ofxnL, tag="oevac")
                st("s", lambda e, tc=tc: e.activation(
                    eq[:, :, tc:tc + 128], of[:, :, tc:tc + 128], AF.Square),
                   reads=[bof], writes=[beq] + AL_gseq, tag="square", no_selfwait=True)

            emit_ktr(0)
            for c in range(4):
                emit_scores(c)
                if c >= 1:
                    emit_tail(c - 1)
                if c < 3:
                    emit_ktr(c + 1)
            emit_tail(3)

            pend.append((m, t0, toff))
        while pend:
            emit_rms(*pend.pop(0))

    def router_chain(m, t0):
                g4 = (128, 4, 8, 4)

                def rv(fn, **kw):
                    st("v", fn, reads=[brt, buf("gbB"), buf("tri")], writes=[brt], **kw)
                rv(lambda e: e.tensor_tensor(
                    scb[:, m], scr[:, m],
                    gbB[:].unsqueeze(1).broadcast_to((128, 4, NE)),
                    ALU.add))
                sc4 = scb[:, m].rearrange("p a (g j) -> p a g j", j=4)
                rv(lambda e: e.tensor_tensor(rt1[:, m], sc4[:, :, :, 0], sc4[:, :, :, 1], ALU.max))
                rv(lambda e: e.tensor_tensor(rt2[:, m], sc4[:, :, :, 2], sc4[:, :, :, 3], ALU.max))
                rv(lambda e: e.tensor_tensor(rt3[:, m], sc4[:, :, :, 0], sc4[:, :, :, 1], ALU.min))
                rv(lambda e: e.tensor_tensor(gsum[:, m], sc4[:, :, :, 2], sc4[:, :, :, 3], ALU.min))
                rv(lambda e: e.tensor_tensor(gsum[:, m], gsum[:, m], rt3[:, m], ALU.max))
                rv(lambda e: e.tensor_tensor(rt3[:, m], rt1[:, m], rt2[:, m], ALU.min))
                rv(lambda e: e.tensor_tensor(rt3[:, m], rt3[:, m], gsum[:, m], ALU.max))
                rv(lambda e: e.tensor_tensor(gsum[:, m], rt1[:, m], rt2[:, m], ALU.max))
                rv(lambda e: e.tensor_tensor(gsum[:, m], gsum[:, m], rt3[:, m], ALU.add))
                for g in range(4):
                    rv(lambda e, g=g: e.max(out=gmax[:, m, g, :], in_=gsum[:, m, g, :]))
                rv(lambda e: e.memset(gmax[:, m, :, 4:8], -1e30))
                for g in range(4):
                    rv(lambda e, g=g: e.match_replace(
                        out=gzap[:, m, g, :], in_to_replace=gmax[:, m, g, :],
                        in_values=gsum[:, m, g, :], imm_value=-1e30))
                rv(lambda e: e.tensor_tensor(
                    gsum[:, m], gsum[:, m], gzap[:, m], ALU.is_gt))
                rv(lambda e: e.tensor_tensor(
                    selm[:, m].rearrange("p a (g j) -> p a g j", j=4),
                    scb[:, m].rearrange("p a (g j) -> p a g j", j=4),
                    gsum[:, m].unsqueeze(3).broadcast_to(g4),
                    ALU.mult))
                for g in range(4):
                    rv(lambda e, g=g: e.max(out=emax[:, m, g, :], in_=selm[:, m, g, :]))
                    rv(lambda e, g=g: e.match_replace(
                        out=ezap[:, m, g, :], in_to_replace=emax[:, m, g, :],
                        in_values=selm[:, m, g, :], imm_value=-1e30))
                rv(lambda e: e.tensor_tensor(
                    selm[:, m].rearrange("p a b -> p (a b)"),
                    selm[:, m].rearrange("p a b -> p (a b)"),
                    ezap[:, m].rearrange("p a b -> p (a b)"), ALU.is_gt))
                rv(lambda e: e.tensor_tensor(
                    tkw[:, m].rearrange("p a b -> p (a b)"),
                    scr[:, m].rearrange("p a b -> p (a b)"),
                    selm[:, m].rearrange("p a b -> p (a b)"), ALU.mult))
                rv(lambda e: e.tensor_reduce(tsum[:, m], tkw[:, m], mybir.AxisListType.X, ALU.add))
                rv(lambda e: e.tensor_scalar(tsum[:, m], tsum[:, m], 1e-20, None, ALU.add))
                rv(lambda e: e.reciprocal(tsum[:, m], tsum[:, m]))
                rv(lambda e: e.tensor_scalar(tsum[:, m], tsum[:, m], 2.5, None, ALU.mult))
                for g in range(4):
                    rv(lambda e, g=g: e.tensor_scalar(
                        combT[:, m, NE * g:NE * (g + 1)], tkw[:, m, g, :], tsum[:, m, g:g + 1],
                        None, ALU.mult))



    def router_comb(m, t0):
                    st("t", [lambda e, g=g: e.transpose(
                        P6b[0:NE, 128 * g:128 * (g + 1)], combT[:, m, NE * g:NE * (g + 1)],
                        ident[:]) for g in range(4)],
                       reads=[brt, buf("ident")], writes=[PB[6]], tag="combtr")
                    cFdst = cF[:, (t0 % SEG):(t0 % SEG) + MT]
                    st("v", lambda e, cFdst=cFdst: e.tensor_copy(cFdst, P6b[0:NE, 0:MT]),
                       reads=[PB[6]], writes=[bcFm[m]])

    def phase_b(seg):
        s0 = seg * SEG
        NT = SEG // MT
        st("g", lambda e: e.dma_start(
            out=WoS[:], in_=Wo_h.rearrange("(kc p) oc -> p kc oc", p=128)),
           writes=[bWo] + AL_Wo, dma="wos", tag="woload")
        for kc in range(NB):
            st("v", lambda e, kc=kc: e.tensor_scalar(
                WoS[:, kc, :], WoS[:, kc, :], nwF[:, kc:kc + 1], None, ALU.mult),
               reads=[bWo, buf("nwF")], writes=[bWo] + AL_Wo)
        for ti in range(NT):
            t0 = s0 + ti * MT
            st("y", lambda e, ti=ti, t0=t0: e.dma_start(
                out=outS[:, ti, :, :],
                in_=x_h[t0:t0 + MT, :].rearrange("(g p) c -> p g c", p=128)),
               writes=[boutS[ti]] + AL_outS, dma="oi%d" % ti, tag="outinit")
            st("y", lambda e, ti=ti, t0=t0: e.dma_start(
                out=onTs[ti][:],
                in_=on_h[:, :, t0:t0 + MT].transpose([1, 0, 2])),
               reads=[bon], writes=[bonT[ti]] + AL_onT[ti], dma="onld%d" % ti, tag="onload")

        jobs = [("sh", shgT_h[:, 0:512], shuT_h[:, 0:512], shdT_h[0:512, :]),
                ("sh", shgT_h[:, 512:1024], shuT_h[:, 512:1024], shdT_h[512:1024, :])]
        for ex in range(NE):
            jobs.append(("e%d" % ex, wgT_h[ex], wuT_h[ex], wdT_h[ex]))

        grps = [jobs[i:i + 2] for i in range(0, len(jobs), 2)]
        jseq = [0]

        for gi, grp in enumerate(grps):
            pars = []
            for js, (kind, gsrc, usrc, dsrc) in enumerate(grp):
                par = jseq[0] % 2
                jseq[0] += 1
                pars.append(par)
                st("g", lambda e, par=par, gsrc=gsrc: e.dma_start(
                    out=wgB[par][:],
                    in_=gsrc.rearrange("(kc p) i -> p kc i", p=128)),
                   writes=[bwg[par]] + AL_wg[par], dma="wg%d" % par, tag="wload")
                st("g", lambda e, par=par, usrc=usrc: e.dma_start(
                    out=wuB[par][:],
                    in_=usrc.rearrange("(kc p) i -> p kc i", p=128)),
                   writes=[bwu[par]] + AL_wu[par], dma="wu%d" % par, tag="wload")
                st("g", lambda e, par=par, dsrc=dsrc: e.dma_start(
                    out=wdB[par][:],
                    in_=dsrc.rearrange("(ic p) hh -> p ic hh", p=128)),
                   writes=[bwd[par]] + AL_wd[par], dma="wd%d" % par, tag="wload")
                is_sh = kind == "sh"
                eidx = None if is_sh else int(kind[1:])
                for ti in range(NT):
                    t0 = s0 + ti * MT
                    bslot = (js * NT + ti) % 2
                    if not is_sh:
                        st("t", lambda e, eidx=eidx, c0=(t0 % SEG): e.matmul(
                            P[4].ap()[:, :], sel32[:, eidx, :],
                            cF[:, c0:c0 + MT], start=True, stop=True),
                           reads=[buf("sel32"), bcFm[(t0 // MT) % NMT]], writes=[PB[4]], tag="bc")
                        st("s", lambda e, bslot=bslot: e.copy(
                            bcS[:, bslot, :], P[4].ap()[:, :]),
                           reads=[PB[4]], writes=[bbc[bslot]] + AL_bc)
                    for ic in range(4):
                        p2 = ic % 2
                        pga, pgb = 4 + 2 * p2, 5 + 2 * p2

                        def gfns(pga=pga, par=par, ic=ic, ti=ti):
                            out = P[pga].ap()[:, :]
                            return [lambda e, kc=kc, out=out: e.matmul(
                                out, wgB[par][:, kc, 128 * ic:128 * (ic + 1)],
                                xnH[:, kc, MT * ti:MT * (ti + 1)],
                                start=(kc == 0), stop=(kc == 7)) for kc in range(NB)]

                        def ufns(pgb=pgb, par=par, ic=ic, ti=ti):
                            out = P[pgb].ap()[:, :]
                            return [lambda e, kc=kc, out=out: e.matmul(
                                out, wuB[par][:, kc, 128 * ic:128 * (ic + 1)],
                                xnH[:, kc, MT * ti:MT * (ti + 1)],
                                start=(kc == 0), stop=(kc == 7)) for kc in range(NB)]
                        st("t", ufns(), reads=[bwu[par], bxnH], writes=[PB[pgb]], tag="umm")
                        st("t", gfns(), reads=[bwg[par], bxnH], writes=[PB[pga]], tag="gmm")
                        st("s", lambda e, pga=pga, p2=p2: e.activation(
                            sgS[:, p2, :], P[pga].ap()[:, :], AF.Silu),
                           reads=[PB[pga]], writes=[bsg[p2]] + AL_sg, tag="silu")
                        mdst = mEt[js][:, ti, ic, :]
                        st("v", lambda e, pgb=pgb, p2=p2, mdst=mdst: e.tensor_tensor(
                            mdst, P[pgb].ap()[:, :], sgS[:, p2, :], ALU.mult),
                           reads=[PB[pgb], bsg[p2]], writes=[bmE[js][ti]] + AL_mE[js], tag="m1")
                        if not is_sh:
                            st("v", lambda e, mdst=mdst, bslot=bslot: e.tensor_tensor(
                                mdst, mdst, bcS[:, bslot, :], ALU.mult),
                               reads=[bbc[bslot], bmE[js][ti]], writes=[bmE[js][ti]] + AL_mE[js], tag="m2")
            if gi == 0:
                for m in range(NMT):
                    router_chain(m, s0 + m * MT)
            for ti in range(NT):
                t0 = s0 + ti * MT
                for hh in range(2):
                    for tg in range(4):
                        def dfns(ti=ti, hh=hh, tg=tg, gi=gi):
                            fs = []
                            first = True
                            if gi == 0:
                                for kc in range(NB):
                                    fs.append(lambda e, kc=kc, ti=ti, hh=hh, tg=tg,
                                              first=first and kc == 0: e.matmul(
                                        P[tg].ap()[:, :],
                                        onTs[ti][:, kc, 128 * tg:128 * (tg + 1)],
                                        WoS[:, kc, 512 * hh:512 * (hh + 1)],
                                        start=(kc == 0), stop=False))
                                first = False
                            for js in range(len(grp)):
                                par = pars[js]
                                for ic in range(4):
                                    fst = first and ic == 0 and js == 0
                                    lst = js == len(grp) - 1 and ic == 3
                                    fs.append(lambda e, js=js, par=par, ic=ic, ti=ti,
                                              hh=hh, tg=tg, fst=fst, lst=lst: e.matmul(
                                        P[tg].ap()[:, :],
                                        mEt[js][:, ti, ic, 128 * tg:128 * (tg + 1)],
                                        wdB[par][:, ic, 512 * hh:512 * (hh + 1)],
                                        start=fst, stop=lst))
                            return fs
                        reads = [bmE[0][ti], bmE[len(grp) - 1][ti],
                                 bwd[pars[0]], bwd[pars[-1]]]
                        if gi == 0:
                            reads += [bonT[ti], bWo]
                        st("t", dfns(), reads=reads, writes=[PB[tg]], tag="dmm")
                    for tg in range(4):
                        st("v", lambda e, ti=ti, hh=hh, tg=tg: e.tensor_tensor(
                            outS[:, ti, tg, 512 * hh:512 * (hh + 1)],
                            outS[:, ti, tg, 512 * hh:512 * (hh + 1)],
                            P[tg].ap()[:, :], ALU.add),
                           reads=[PB[tg], boutS[ti]], writes=[boutS[ti]] + AL_outS, tag="devac")
            if gi == 0:
                for m in range(NMT):
                    router_comb(m, s0 + m * MT)
        dump("B_outS", outS[:, 0, :, :], [boutS[0]])
        dump("B_mE", mE0[:, :, :, :], [bmE[0][0], bmE[0][1]])
        for ti in range(NT):
            t0 = s0 + ti * MT
            st("y", lambda e, ti=ti, t0=t0: e.dma_start(
                out=out_h[t0:t0 + MT, :].rearrange("(g p) c -> p g c", p=128),
                in_=outS[:, ti, :, :]),
               reads=[boutS[ti]], writes=[bout], dma="outst", tag="store")

    for seg in range(NSEG):
        phase_a(seg)
        phase_b(seg)
    mk.finalize()
    mk.emit()
    nc._mk_sched = mk
    return nc



_CACHE = {}


def _layout(inputs, T):
    C = np.ascontiguousarray
    return {
        "Wq": inputs["L0_attn_Wq"], "Wk": inputs["L0_attn_Wk"],
        "Wv": inputs["L0_attn_Wv"], "Wo": inputs["L0_attn_Wo"],
        "delta": inputs["L0_attn_delta"], "G1": inputs["L0_attn_G1"],
        "G2": inputs["L0_attn_G2"], "norm_w": inputs["L0_attn_norm_w"],
        "gwT": C(np.asarray(inputs["L0_ffn_gate_w"]).T),
        "gate_bias": inputs["L0_ffn_gate_bias"],
        "wgT": C(np.asarray(inputs["L0_ffn_w_gate"]).transpose(0, 2, 1)),
        "wuT": C(np.asarray(inputs["L0_ffn_w_up"]).transpose(0, 2, 1)),
        "wdT": C(np.asarray(inputs["L0_ffn_w_down"]).transpose(0, 2, 1)),
        "shgT": C(np.asarray(inputs["L0_ffn_sh_gate"]).T),
        "shuT": C(np.asarray(inputs["L0_ffn_sh_up"]).T),
        "shdT": C(np.asarray(inputs["L0_ffn_sh_down"]).T),
    }


def kernel(**inputs):
    x = np.asarray(inputs["L0_attn_x"])
    B, T, _ = x.shape
    nc = build(T=T, SEG=min(1024, T))
    w = _layout(inputs, T)
    in_maps = [dict(w, x=np.ascontiguousarray(x[b])) for b in range(B)]
    res = run_bass_kernel_spmd(nc, in_maps, core_ids=list(range(B)))
    return np.stack([np.asarray(r["out"]) for r in res.results], axis=0)


# version 17
# speedup vs baseline: 1.0248x; 1.0248x over previous
import numpy as np
import concourse.bass as bass
import concourse.mybir as mybir
from concourse.bass_utils import run_bass_kernel_spmd

F32 = mybir.dt.float32
BF16 = mybir.dt.bfloat16
I32 = mybir.dt.int32
U8 = mybir.dt.uint8
AF = mybir.ActivationFunctionType
ALU = mybir.AluOpType

E = 1024
NH = 16
HD = 64
NB = 8
MT = 512
CH = 128
NE = 32
EI = 512
EPS = 1e-5



class Buf:
    __slots__ = ("w", "r", "name")

    def __init__(self, name=""):
        self.w = []
        self.r = []
        self.name = name


class MK:
    ENGS = ("t", "v", "s", "g", "y")

    SERIAL = False

    def __init__(self, nc):
        self.nc = nc
        self.eng = {"t": nc.tensor, "v": nc.vector, "s": nc.scalar,
                    "g": nc.gpsimd, "y": nc.sync}
        self.sem = {k: nc.alloc_semaphore(name=f"ms_{k}") for k in self.ENGS}
        self.cnt = {k: 0 for k in self.ENGS}
        self.steps = {k: [] for k in self.ENGS}
        self.dsem = {}

    def dma(self, name):
        if name not in self.dsem:
            self.dsem[name] = [self.nc.alloc_semaphore(name=f"dma_{name}"), 0]
        return self.dsem[name]

    def step(self, ek, fns, reads=(), writes=(), dma=None, tag="", no_selfwait=False):
        if not isinstance(fns, (list, tuple)):
            fns = [fns]
        deps = {}
        selfw = [0]

        def add(tok):
            sem, val = tok
            if sem is self.sem[ek]:
                if val >= self.cnt[ek] - 1:
                    selfw[0] = max(selfw[0], val)
                return
            k = sem.num
            if k not in deps or deps[k][1] < val:
                deps[k] = (sem, val)

        for b in reads:
            for t in b.w:
                add(t)
        for b in writes:
            for t in b.w:
                add(t)
            for t in b.r:
                add(t)
        if self.SERIAL:
            for k2 in self.ENGS:
                if self.cnt[k2]:
                    add((self.sem[k2], self.cnt[k2]))
            for (h, c) in self.dsem.values():
                if c:
                    add((h, c))
        if selfw[0] and not no_selfwait:
            deps[self.sem[ek].num] = (self.sem[ek], selfw[0])
        if dma is None:
            self.cnt[ek] += 1
            tok = (self.sem[ek], self.cnt[ek])
            inc = (self.sem[ek], 1)
        else:
            ds = self.dma(dma)
            ds[1] += 16
            tok = (ds[0], ds[1])
            inc = (ds[0], 16)
        self.steps[ek].append((list(deps.values()), fns, inc, tag))
        for b in reads:
            b.r.append(tok)
        for b in writes:
            b.w = [tok]
            b.r = []
        return tok

    def barrier(self, skip=()):
        toks = [(self.sem[k], self.cnt[k]) for k in self.ENGS]
        toks += [(h, c) for (h, c) in self.dsem.values()]
        for k in self.ENGS:
            if k in skip:
                continue
            deps = [(s, v) for (s, v) in toks if s is not self.sem[k]]
            self.cnt[k] += 1
            self.steps[k].append(
                (deps, [lambda e: e.nop()], (self.sem[k], 1), "barrier"))

    def finalize(self):
        self.barrier()

    def emit(self):
        with self.nc.Block() as block:
            for k in self.ENGS:
                def runner(e, k=k):
                    seen = {}
                    for deps, fns, inc, tag in self.steps[k]:
                        for sem, val in deps:
                            if seen.get(sem.num, -1) < val:
                                e.wait_ge(sem, val)
                                seen[sem.num] = val
                        inst = None
                        for f in fns:
                            inst = f(e)
                        inst.then_inc(inc[0], inc[1])
                getattr(block, {"t": "tensor", "v": "vector", "s": "scalar",
                                "g": "gpsimd", "y": "sync"}[k])(runner)



def build(T=4096, SEG=1024, dbg=()):
    assert T % SEG == 0 and SEG % MT == 0
    NSEG = T // SEG
    NMT = SEG // MT

    nc = bass.Bass("TRN2", target_bir_lowering=False)
    mk = MK(nc)

    dt_in = {}

    def din(name, shape, dtype=F32):
        h = nc.dram_tensor(name, list(shape), dtype, kind="ExternalInput").ap()
        dt_in[name] = h
        return h

    x_h = din("x", (T, E))
    Wq_h = din("Wq", (E, E)); Wk_h = din("Wk", (E, E)); Wv_h = din("Wv", (E, E))
    Wo_h = din("Wo", (E, E))
    dlt_h = din("delta", (E,))
    G1_h = din("G1", (E, HD)); G2_h = din("G2", (HD, E))
    nw_h = din("norm_w", (E,))
    gwT_h = din("gwT", (E, NE))
    gb_h = din("gate_bias", (NE,))
    wgT_h = din("wgT", (NE, E, EI)); wuT_h = din("wuT", (NE, E, EI))
    wdT_h = din("wdT", (NE, EI, E))
    shgT_h = din("shgT", (E, E)); shuT_h = din("shuT", (E, E))
    shdT_h = din("shdT", (E, E))

    out_h = nc.dram_tensor("out", [T, E], F32, kind="ExternalOutput").ap()
    on_h = nc.dram_tensor("on_scr", [NB, 128, T], BF16, kind="Internal").ap()

    P = [nc.alloc_psum_tensor(f"bank{i}", [128, 512], F32) for i in range(8)]
    PB = [Buf(f"P{i}") for i in range(8)]
    P6b = P[6].ap().bitcast(BF16)
    P7b = P[7].ap().bitcast(BF16)

    def sb(name, shape, dtype):
        return nc.alloc_sbuf_tensor(name, list(shape), dtype)

    xnH = sb("xnH", (128, NB, SEG), BF16)
    cF = sb("cF", (NE, SEG), BF16)
    SA = sb("SA", (128, NB, 128), F32)
    SBf = sb("SBf", (128, NB, 128), BF16)
    kT = sb("kT", (128, 2, E), BF16)
    Abar = sb("Abar", (128, 2, NH, 128), BF16)
    ECb = sb("ECb", (128, NB, 4), F32)
    tri = sb("tri", (128, 128), F32)
    ident = sb("ident", (128, 128), BF16)
    ones1 = sb("ones1", (1, 128), BF16)
    ones1f = sb("ones1f", (1, 128), F32)
    sel32 = sb("sel32", (NE, NE, 128), BF16)
    id32 = sb("id32", (NE, NE), BF16)
    id16 = sb("id16", (16, 16), BF16)
    hi2 = sb("hi2", (128, 2), BF16)
    sel16 = sb("sel16", (128, NB, 16), BF16)
    selB = sb("selB", (16, NB, 128), BF16)
    posd = sb("posd", (128, NB), F32)
    negd = sb("negd", (128, NB), F32)
    nwF = sb("nwF", (128, NB), F32)
    gwF = sb("gwF", (128, NB, NE), F32)
    gwH = sb("gwH", (128, NB, NE), BF16)
    gwL = sb("gwL", (128, NB, NE), BF16)
    gb1 = sb("gb1", (1, NE), F32)
    gbB = sb("gbB", (128, NE), F32)
    epsH = sb("epsH", (16, 1), F32)
    epsLN = sb("epsLN", (128, 1), F32)
    G1b = sb("G1b", (128, NB, HD), BF16)
    G2b = sb("G2b", (HD, E), BF16)
    rsH = sb("rsH", (16, 512), BF16)
    rsL = sb("rsL", (16, 512), F32)
    bnst = sb("bnst", (128, 12), F32)
    bnag = sb("bnag", (128, 2), F32)
    lnt = sb("lnt", (128, 1), F32)
    scr = sb("scr", (128, 2, 4, NE), F32)
    scb = sb("scb", (128, 2, 4, NE), F32)
    gsum = sb("gsum", (128, 2, 4, 8), F32)
    gmax = sb("gmax", (128, 2, 4, 8), F32)
    gzap = sb("gzap", (128, 2, 4, 8), F32)
    emax = sb("emax", (128, 2, 4, 8), F32)
    ezap = sb("ezap", (128, 2, 4, NE), F32)
    selm = sb("selm", (128, 2, 4, NE), F32)
    tkw = sb("tkw", (128, 2, 4, NE), F32)
    tsum = sb("tsum", (128, 2, 4), F32)
    rt1 = sb("rt1", (128, 2, 4, 8), F32)
    rt2 = sb("rt2", (128, 2, 4, 8), F32)
    rt3 = sb("rt3", (128, 2, 4, 8), F32)
    combT = sb("combT", (128, 2, 128), BF16)

    ARENA = 147456
    arena = nc.alloc_sbuf_tensor("arena", [128, ARENA], U8)
    arena_addr = nc.lookup_mloc(arena).addr

    def place(name, shape, dtype, off):
        per = int(np.prod(shape[1:])) * mybir.dt.size(dtype)
        assert off + per <= ARENA, (name, off, per)
        h = nc.virtual_sbuf_tensor(name, list(shape), dtype)
        nc._tensor(h.name, h.shape, h.dtype, type="SB")
        mloc = nc.lookup_mloc(h)
        mloc.allocated = True
        mloc.addr = arena_addr + off
        return h

    xA = place("xA", (128, 2, 1024), F32, 0)
    WkB = place("WkB", (128, NB, E), BF16, 8192)
    tmA = place("tmA", (128, 1024), BF16, 24576)
    tmB = place("tmB", (128, 1024), BF16, 26624)
    xg1 = place("xg1", (HD, 512), BF16, 28672)
    WvB = place("WvB", (128, NB, E), BF16, 32768)
    WqB = place("WqB", (128, NB, E), BF16, 49152)
    fg = place("fg", (128, NB, 512), F32, 65536)
    qsil = place("qsil", (128, NB, 512), BF16, 81920)
    kk = place("kk", (128, NB, 512), BF16, 90112)
    vv = place("vv", (128, 4, 1024), BF16, 98304)
    gsb = place("gsb", (128, NB, 512), BF16, 106496)
    eq = place("eq", (128, NB, 512), BF16, 114688)
    of = place("of", (128, NB, 512), BF16, 122880)
    xnL = place("xnL", (128, NB, 512), BF16, 131072)
    ek = place("ek", (128, NB, 512), BF16, 131072)
    onT0 = place("onT0", (128, NB, 512), BF16, 0)
    WoS = place("WoS", (128, NB, E), BF16, 8192)
    onT1 = place("onT1", (128, NB, 512), BF16, 24576)
    wgB1 = place("wgB1", (128, NB, 512), BF16, 32768)
    wuB1 = place("wuB1", (128, NB, 512), BF16, 40960)
    mE0 = place("mE0", (128, 2, 4, 512), BF16, 49152)
    wgB0 = place("wgB0", (128, NB, 512), BF16, 57344)
    mE1 = place("mE1", (128, 2, 4, 512), BF16, 65536)
    wuB0 = place("wuB0", (128, NB, 512), BF16, 73728)
    wdB0 = place("wdB0", (128, 4, 1024), BF16, 81920)
    wdB1 = place("wdB1", (128, 4, 1024), BF16, 90112)
    sgS = place("sgS", (128, 2, 512), BF16, 98304)
    bcS = place("bcS", (128, 2, 512), BF16, 100352)
    outS = place("outS", (128, 2, 4, 1024), F32, 114688)
    iotaI = place("iotaI", (128, 128), I32, 139264)
    wgB = [wgB0, wgB1]; wuB = [wuB0, wuB1]; wdB = [wdB0, wdB1]
    onTs = [onT0, onT1]; mEt = [mE0, mE1]

    rem = nc.sbuf_bytes_remaining
    assert rem >= 0, f"SBUF over budget by {-rem} bytes"

    _dumped = set()

    def dump(tag, src_ap, read_bufs):
        if tag not in dbg or tag in _dumped:
            return
        _dumped.add(tag)
        h = nc.dram_tensor("dbg_" + tag, list(src_ap.shape), src_ap.dtype,
                           kind="ExternalOutput").ap()
        idx = tuple(slice(None) for _ in src_ap.shape)
        mk.step("y", lambda e: e.dma_start(out=h[idx], in_=src_ap),
                reads=list(read_bufs), dma="dbg_%s" % tag, tag="dbg")

    B_ = {}

    def buf(name):
        if name not in B_:
            B_[name] = Buf(name)
        return B_[name]

    st = mk.step
    st("g", lambda e: e.iota(iotaI[:], pattern=[[-1, 128]], base=0,
                             channel_multiplier=1), writes=[buf("iota")])
    st("v", lambda e: e.tensor_scalar(tri[:], iotaI[:], 0.0, None, ALU.is_le),
       reads=[buf("iota")], writes=[buf("tri")])
    st("v", lambda e: e.tensor_scalar(ident[:], iotaI[:], 0.0, None, ALU.is_equal),
       reads=[buf("iota")], writes=[buf("ident")])
    st("v", lambda e: e.memset(ones1[:], 1.0), writes=[buf("ones1")])
    st("v", lambda e: e.memset(ones1f[:], 1.0), writes=[buf("ones1f")])
    st("v", lambda e: e.memset(epsH[:], EPS), writes=[buf("epsH")])
    st("v", lambda e: e.memset(epsLN[:], EPS), writes=[buf("epsLN")])
    st("v", lambda e: e.memset(SA[:], 0.0), writes=[buf("SA")])
    st("v", lambda e: e.memset(SBf[:], 0.0), writes=[buf("SBf")])
    st("v", lambda e: e.memset(sel16[:], 0.0), writes=[buf("sel16")])
    st("v", lambda e: e.tensor_scalar(id32[:], iotaI[0:NE, 0:NE], 0.0, None, ALU.is_equal),
       reads=[buf("iota")], writes=[buf("id32")])
    st("v", lambda e: e.tensor_copy(
        sel32[:], id32[:].unsqueeze(2).broadcast_to((NE, NE, 128))),
       reads=[buf("id32")], writes=[buf("sel32")])
    st("v", lambda e: e.tensor_scalar(id16[:], iotaI[0:16, 0:16], 0.0, None, ALU.is_equal),
       reads=[buf("iota")], writes=[buf("id16")])
    st("v", lambda e: e.tensor_copy(
        selB[:].rearrange("k b (h r) -> k b h r", h=2),
        id16[:].rearrange("k (b h) -> k b h", h=2).unsqueeze(3).broadcast_to((16, NB, 2, 64))),
       reads=[buf("id16")], writes=[buf("selB")])
    st("v", lambda e: e.tensor_scalar(hi2[:, 0:1], iotaI[:, 0:1], 64.0, None, ALU.is_lt),
       reads=[buf("iota")], writes=[buf("hi2")])
    st("v", lambda e: e.tensor_scalar(hi2[:, 1:2], iotaI[:, 0:1], 63.0, None, ALU.is_gt),
       reads=[buf("iota")], writes=[buf("hi2")])
    sel16f = sel16[:].rearrange("p a b -> p (a b)")
    for h2 in range(2):
        import dataclasses as _dc
        dst = _dc.replace(sel16f, offset=sel16f.offset + h2,
                          ap=[list(sel16f.ap)[0], [18, 8]])
        st("v", lambda e, dst=dst, h2=h2: e.tensor_copy(
            dst, hi2[:, h2:h2 + 1].broadcast_to((128, 8))),
           reads=[buf("hi2")], writes=[buf("sel16")])
    st("y", lambda e: e.dma_start(out=posd[:], in_=dlt_h.rearrange("(b p) -> p b", p=128), allow_slow_non_contiguous=True),
       writes=[buf("posd")], dma="ldposd")
    st("v", lambda e: e.tensor_scalar(negd[:], posd[:], -1.0, None, ALU.mult),
       reads=[buf("posd")], writes=[buf("negd")])
    st("y", lambda e: e.dma_start(out=nwF[:], in_=nw_h.rearrange("(b p) -> p b", p=128), allow_slow_non_contiguous=True),
       writes=[buf("nwF")], dma="ldnw")
    st("y", lambda e: e.dma_start(out=gwF[:], in_=gwT_h.rearrange("(b p) ne -> p b ne", p=128)),
       writes=[buf("gwF")], dma="ldgw")
    st("v", lambda e: e.tensor_copy(gwH[:], gwF[:]), reads=[buf("gwF")], writes=[buf("gwH")])
    st("v", lambda e: e.scalar_tensor_tensor(gwL[:], gwF[:], 0.0, gwH[:],
                                             ALU.bypass, ALU.subtract),
       reads=[buf("gwF"), buf("gwH")], writes=[buf("gwL")])
    st("y", lambda e: e.dma_start(out=gb1[:], in_=gb_h[None, :]),
       writes=[buf("gb1")], dma="ldgb")
    st("t", lambda e: e.matmul(P[0].ap()[:, 0:NE], ones1f[0:1, :], gb1[0:1, :],
                               start=True, stop=True),
       reads=[buf("ones1f"), buf("gb1")], writes=[PB[0]])
    st("v", lambda e: e.tensor_copy(gbB[:], P[0].ap()[:, 0:NE]),
       reads=[PB[0]], writes=[buf("gbB")])
    st("g", lambda e: e.dma_start(out=G1b[:], in_=G1_h.rearrange("(b p) d -> p b d", p=128)),
       writes=[buf("G1b")], dma="ldg1")
    st("g", lambda e: e.dma_start(out=G2b[:], in_=G2_h[:, :]),
       writes=[buf("G2b")], dma="ldg2")

    bxA = [buf("xA0"), buf("xA1")]
    bfg = buf("fg"); bqs = buf("qsil"); bkk = buf("kk"); bvv = buf("vv")
    bgs = buf("gsb"); beq = buf("eq"); bof = buf("of")
    bxnL = buf("xnL"); bek = bxnL; btmA = buf("tmA"); btmB = buf("tmB"); bxg1 = buf("xg1")
    bWq = buf("WqB"); bWk = buf("WkB"); bWv = buf("WvB")
    bxnH = buf("xnH"); bcFm = [buf("cF0"), buf("cF1")]; bSA = buf("SA"); bSBf = buf("SBf")
    bkTs = [buf("kT0"), buf("kT1")]
    bAbw = [[buf("Abar00"), buf("Abar01")], [buf("Abar10"), buf("Abar11")]]
    bEC = buf("ECb")
    brt = buf("rt")
    brsH = buf("rsH"); brsL = buf("rsL"); bbn = buf("bn")
    bon = buf("on_scr")
    bmE = [[buf("mE%d%d" % (a, b)) for b in range(2)] for a in range(2)]
    bwg = [buf("wg0"), buf("wg1")]
    bwu = [buf("wu0"), buf("wu1")]; bwd = [buf("wd0"), buf("wd1")]
    bWo = buf("WoS"); bonT = [buf("onT0"), buf("onT1")]
    boutS = [buf("outS0"), buf("outS1")]
    bsg = [buf("sg0"), buf("sg1")]; bbc = [buf("bc0"), buf("bc1")]
    bout = buf("out_h")

    AL_onT = [[bxA[0], bxA[1]], [btmA, btmB, bxg1]]
    AL_wg = [[bWq], [bWv]]
    AL_wu = [[bfg], [bWv]]
    AL_wd = [[bqs], [bkk]]
    AL_Wo = [bWk]
    AL_mE = [[bWq], [bfg]]
    AL_outS = [beq, bof, bxnL, buf("iota")]
    AL_sg = [bvv]
    AL_bc = [bvv]
    AL_xA = [bonT[0]]
    AL_fg = bmE[1] + [bwu[0]]
    AL_qsil = [bwd[0]]
    AL_kk = [bwd[1]]
    AL_vv = [bsg[0], bsg[1], bbc[0], bbc[1]]
    AL_gseq = [boutS[0], boutS[1]]
    AL_ofxnL = [boutS[0], boutS[1]]
    AL_tmA = [bonT[1]]
    AL_tmB = [bonT[1]]
    AL_WqB = bmE[0] + [bwg[0]]
    AL_WkB = [bWo]
    AL_WvB = [bwg[1], bwu[1]]

    def phase_a(seg):
        s0 = seg * SEG
        pend = []
        def emit_rms(m, t0, toff):
                st("t", [lambda e, blk=blk: e.matmul(
                    P[0].ap()[0:16, :], sel16[:, blk, :], eq[:, blk, :],
                    start=(blk == 0), stop=(blk == 7)) for blk in range(NB)],
                   reads=[beq, buf("sel16")], writes=[PB[0]], tag="rmssum")
                st("s", lambda e: e.activation(rsL[:], P[0].ap()[0:16, :], AF.Ln,
                                               bias=epsH[:], scale=1.0 / HD),
                   reads=[PB[0], buf("epsH")], writes=[brsL], tag="rms_rs")
                st("s", lambda e: e.activation(rsH[:], rsL[:], AF.Exp, scale=-0.5),
                   reads=[brsL], writes=[brsH], tag="rms_rs")
                for blk in range(NB):
                    pbk = 2 + blk % 2
                    st("t", lambda e, blk=blk, pbk=pbk: e.matmul(
                        P[pbk].ap()[:, :], selB[:, blk, :], rsH[:],
                        start=True, stop=True),
                       reads=[brsH, buf("selB")], writes=[PB[pbk]], tag="rsbc")
                    st("v", lambda e, blk=blk, pbk=pbk: e.tensor_tensor(
                        of[:, blk, :], of[:, blk, :], P[pbk].ap()[:, :], ALU.mult),
                       reads=[PB[pbk], bof], writes=[bof] + AL_ofxnL, tag="onmul")
                dump("A_of", of[:, :, :], [bof])
                dump("A_SA", SA[:, :, :], [bSA])
                dump("A_Abar", Abar[:, 1, :, :], [bAbw[1][0], bAbw[1][1]])
                ondst = on_h[:, :, t0:t0 + MT].transpose([1, 0, 2])
                st("y", lambda e, ondst=ondst: e.dma_start(out=ondst, in_=of[:]),
                   reads=[bof], writes=[bon], dma="onst", tag="onstore")

        for nm, W_h, bW, dst in (("wqs", Wq_h, bWq, WqB), ("wks", Wk_h, bWk, WkB),
                                 ("wvs", Wv_h, bWv, WvB)):
            st("g", lambda e, W_h=W_h, dst=dst: e.dma_start(
                out=dst[:], in_=W_h.rearrange("(kc p) oc -> p kc oc", p=128)),
               writes=[bW] + (AL_WqB if dst is WqB else AL_WkB if dst is WkB else AL_WvB), dma=nm, tag="wqkv")

        for m in range(NMT):
            t0 = s0 + m * MT
            toff = m * MT
            for g in range(4):
                slot = g % 2
                xa = xA[:, slot, :]
                xsrc = x_h[t0 + 128 * g: t0 + 128 * (g + 1), :]
                st("y", lambda e, xa=xa, xsrc=xsrc: e.dma_start(out=xa, in_=xsrc),
                   writes=[bxA[slot]] + AL_xA, dma="xin%d" % slot, tag="xload")
                st("v", [lambda e, xa=xa: e.bn_stats(bnst[:, 0:6], xa[:, 0:512]),
                         lambda e, xa=xa: e.bn_stats(bnst[:, 6:12], xa[:, 512:1024])],
                   reads=[bxA[slot]], writes=[bbn])
                st("v", lambda e: e.bn_aggr(bnag[:], bnst[:]),
                   reads=[bbn], writes=[bbn])
                st("s", lambda e: e.activation(lnt[:], bnag[:, 1:2], AF.Ln, bias=epsLN[:]),
                   reads=[bbn, buf("epsLN")], writes=[buf("lnt")])
                st("s", lambda e: e.activation(lnt[:], lnt[:], AF.Exp, scale=-0.5),
                   reads=[buf("lnt")], writes=[buf("lnt")])
                st("v", lambda e, xa=xa: e.tensor_scalar(
                    xa, xa, bnag[:, 0:1], lnt[:], ALU.subtract, ALU.mult),
                   reads=[bbn, buf("lnt")], writes=[bxA[slot]] + AL_xA)
                st("s", lambda e, xa=xa: e.copy(tmA[:], xa),
                   reads=[bxA[slot]], writes=[btmA] + AL_tmA)
                st("v", lambda e, xa=xa: e.scalar_tensor_tensor(
                    tmB[:], xa, 0.0, tmA[:], ALU.bypass, ALU.subtract),
                   reads=[bxA[slot], btmA], writes=[btmB] + AL_tmB)
                st("t", [lambda e, b=b: e.transpose(
                    P6b[:, 128 * b:128 * (b + 1)], tmA[:, 128 * b:128 * (b + 1)], ident[:])
                    for b in range(NB)],
                   reads=[btmA, buf("ident")], writes=[PB[6]], tag="xntr")
                st("t", [lambda e, b=b: e.transpose(
                    P7b[:, 128 * b:128 * (b + 1)], tmB[:, 128 * b:128 * (b + 1)], ident[:])
                    for b in range(NB)],
                   reads=[btmB, buf("ident")], writes=[PB[7]], tag="xntr")
                xnHdst = xnH[:, :, toff + 128 * g: toff + 128 * (g + 1)]
                st("s", lambda e, xnHdst=xnHdst: e.copy(
                    xnHdst, P6b[:].rearrange("p (b t) -> p b t", b=NB)),
                   reads=[PB[6]], writes=[bxnH])
                st("s", lambda e, g=g: e.copy(
                    xnL[:, :, 128 * g: 128 * (g + 1)],
                    P7b[:].rearrange("p (b t) -> p b t", b=NB)),
                   reads=[PB[7]], writes=[bxnL] + AL_ofxnL)

            xnHm = xnH[:, :, toff:toff + MT]
            def rfns():
                fs = []
                for g in range(4):
                    out = P[2].ap()[:, NE * g:NE * (g + 1)]
                    for i, (lh, rh) in enumerate(
                            ((xnHm, gwH), (xnHm, gwL), (xnL, gwH))):
                        for kc in range(NB):
                            lhs = lh[:, kc, 128 * g:128 * (g + 1)]
                            fs.append(lambda e, out=out, lhs=lhs, rh=rh, kc=kc,
                                      first=(i == 0 and kc == 0),
                                      last=(i == 2 and kc == 7): e.matmul(
                                out, lhs, rh[:, kc, :], start=first, stop=last))
                return fs
            st("t", rfns(), reads=[bxnH, bxnL, buf("gwH"), buf("gwL")],
               writes=[PB[2]], tag="router_mm")
            scrdst = scr[:, m, :, :].rearrange("p a b -> p (a b)")
            st("s", lambda e, scrdst=scrdst: e.activation(
                scrdst, P[2].ap()[:, 0:128], AF.Sigmoid),
               reads=[PB[2]], writes=[brt], tag="sig")
            while pend and pend[0][0] < m:
                emit_rms(*pend.pop(0))
            xnHm = xnH[:, :, toff:toff + MT]

            def proj_fm(dstW, bW, pb, blk, extra_reads):
                def fns(xnHm=xnHm):
                    out = P[pb].ap()[:, :]
                    return [lambda e, kc=kc, out=out, xnHm=xnHm: e.matmul(
                        out, dstW[:, kc, 128 * blk:128 * (blk + 1)],
                        xnHm[:, kc, :], start=(kc == 0), stop=(kc == 7))
                        for kc in range(NB)]
                return st("t", fns(), reads=[bW, bxnH] + extra_reads,
                          writes=[PB[pb]], tag="proj")

            for blk in range(NB):
                pb = blk % 4
                proj_fm(WkB, bWk, pb, blk, [])
                st("s", lambda e, pb=pb, blk=blk: e.activation(
                    kk[:, blk, :], P[pb].ap()[:, :], AF.Sigmoid,
                    bias=negd[:, blk:blk + 1], scale=-1.0),
                   reads=[PB[pb], buf("negd")], writes=[bkk] + AL_kk, tag="sig")
                st("s", lambda e, pb=pb, blk=blk: e.activation(
                    fg[:, blk, :], P[pb].ap()[:, :], AF.Sigmoid,
                    bias=posd[:, blk:blk + 1], scale=1.0),
                   reads=[PB[pb], buf("posd")], writes=[bfg] + AL_fg, tag="sig")
            st("t", [lambda e, kc=kc, xnHm=xnHm: e.matmul(
                P[0].ap()[0:HD, :], G1b[:, kc, :], xnHm[:, kc, :],
                start=(kc == 0), stop=(kc == 7)) for kc in range(NB)],
               reads=[buf("G1b"), bxnH], writes=[PB[0]], tag="xg1")
            st("v", lambda e: e.tensor_copy(xg1[:], P[0].ap()[0:HD, :]),
               reads=[PB[0]], writes=[bxg1, bonT[1]])
            for blk in range(NB):
                pb = blk % 4
                st("t", lambda e, blk=blk, pb=pb: e.matmul(
                    P[pb].ap()[:, :], G2b[:, 128 * blk:128 * (blk + 1)], xg1[:],
                    start=True, stop=True),
                   reads=[buf("G2b"), bxg1], writes=[PB[pb]], tag="gate")
                st("s", lambda e, pb=pb, blk=blk: e.activation(
                    gsb[:, blk, :], P[pb].ap()[:, :], AF.Sigmoid),
                   reads=[PB[pb]], writes=[bgs], tag="sig")
            st("s", lambda e: e.activation(
                fg[:].rearrange("p a b -> p (a b)"),
                fg[:].rearrange("p a b -> p (a b)"), AF.Ln),
               reads=[bfg], writes=[bfg] + AL_fg, tag="ln_g")
            for blk in range(NB):
                for c in range(4):
                    sl = fg[:, blk, 128 * c:128 * (c + 1)]
                    st("v", lambda e, sl=sl: e.tensor_tensor_scan(
                        sl, sl, sl, 0.0, ALU.add, ALU.bypass),
                       reads=[bfg], writes=[bfg] + AL_fg, no_selfwait=True)
            st("s", lambda e: e.activation(
                ECb[:].rearrange("p a b -> p (a b)"),
                fg[:, :, 127::128].rearrange("p a b -> p (a b)"), AF.Exp),
               reads=[bfg], writes=[bEC], tag="exp")
            st("s", lambda e: e.activation(
                eq[:].rearrange("p a b -> p (a b)"),
                fg[:].rearrange("p a b -> p (a b)"), AF.Exp, scale=1.0),
               reads=[bfg], writes=[beq] + AL_gseq, tag="exp")
            st("s", lambda e: e.activation(
                ek[:].rearrange("p a b -> p (a b)"),
                fg[:].rearrange("p a b -> p (a b)"), AF.Exp, scale=-1.0),
               reads=[bfg], writes=[bek] + AL_ofxnL, tag="exp")
            st("v", lambda e: e.tensor_tensor(
                kk[:].rearrange("p a b -> p (a b)"),
                kk[:].rearrange("p a b -> p (a b)"),
                ek[:].rearrange("p a b -> p (a b)"), ALU.mult),
               reads=[bkk, bek], writes=[bkk] + AL_kk)
            for blk in range(NB):
                pb = blk % 4
                proj_fm(WqB, bWq, pb, blk, [])
                st("s", lambda e, pb=pb, blk=blk: e.activation(
                    qsil[:, blk, :], P[pb].ap()[:, :], AF.Silu),
                   reads=[PB[pb]], writes=[bqs] + AL_qsil, tag="qsilu")
            st("v", lambda e: e.tensor_tensor(
                qsil[:].rearrange("p a b -> p (a b)"),
                qsil[:].rearrange("p a b -> p (a b)"),
                eq[:].rearrange("p a b -> p (a b)"), ALU.mult),
               reads=[bqs, beq], writes=[bqs] + AL_qsil)


            dump("A_xn", xnH[:, :, toff:toff + MT], [bxnH])
            dump("A_bnag", bnag[:, :], [bbn])
            dump("A_lnt", lnt[:, :], [buf("lnt")])
            dump("A_xa", xA[:, :, :], [bxA[0], bxA[1]])
            dump("A_xnL", xnL[:, :, :], [bxnL])
            dump("A_B", fg[:, :, :], [bfg])
            dump("A_q", qsil[:, :, :], [bqs])
            dump("A_k", kk[:, :, :], [bkk])
            dump("A_v", vv[:, :, :], [bvv])
            dump("A_gs", gsb[:, :, :], [bgs])
            dump("A_eq", eq[:, :, :], [beq])
            dump("A_EC", ECb[:, :, :], [bEC])
            dump("A_cF", cF[:, 0:MT], [bcFm[0]])
            for g in range(4):
                for oh in range(2):
                    pb = (g * 2 + oh) % 4

                    def fns(g=g, oh=oh, pb=pb, xnHm=xnHm):
                        out = P[pb].ap()[:, :]
                        return [lambda e, kc=kc, g=g, oh=oh, out=out, xnHm=xnHm:
                                e.matmul(
                            out, xnHm[:, kc, 128 * g:128 * (g + 1)],
                            WvB[:, kc, 512 * oh:512 * (oh + 1)],
                            start=(kc == 0), stop=(kc == 7)) for kc in range(NB)]
                    st("t", fns(), reads=[bWv, bxnH], writes=[PB[pb]], tag="vproj")
                    st("v", lambda e, pb=pb, g=g, oh=oh: e.tensor_copy(
                        vv[:, g, 512 * oh:512 * (oh + 1)], P[pb].ap()[:, :]),
                       reads=[PB[pb]], writes=[bvv] + AL_vv)
            def emit_ktr(c):
                tc = 128 * c
                st("t", [lambda e, b=b, src=kk[:, b, tc:tc + 128]: e.transpose(
                    P6b[:, 128 * b:128 * (b + 1)], src, ident[:])
                    for b in range(NB)],
                   reads=[bkk, buf("ident")], writes=[PB[6]], tag="ktr")
                st("s", lambda e, sl=kT[:, c % 2, :]: e.copy(sl, P6b[:]),
                   reads=[PB[6]], writes=[bkTs[c % 2]], tag="ktev")

            def emit_scores(c):
                tc = 128 * c
                par = c % 2
                for w in range(2):
                    pb0 = 2 - 2 * w
                    def sfns(w=w, pb0=pb0, tc=tc):
                        fs = []
                        for j in range(4 * w, 4 * w + 4):
                            col = 128 * (j % 4)
                            for hh2 in range(2):
                                pbk = pb0 + hh2
                                lhs = kk[64 * hh2:64 * (hh2 + 1), j, tc:tc + 128]
                                rhs = qsil[64 * hh2:64 * (hh2 + 1), j, tc:tc + 128]
                                fs.append(lambda e, pbk=pbk, col=col, lhs=lhs, rhs=rhs:
                                          e.matmul(P[pbk].ap()[:, col:col + 128],
                                                   lhs, rhs, start=True, stop=True))
                        return fs
                    st("t", sfns(), reads=[bkk, bqs], writes=[PB[pb0], PB[pb0 + 1]],
                       tag="score")
                    for hh2 in range(2):
                        pbk = pb0 + hh2
                        hsel = Abar[:, par, 8 * w + hh2: 8 * w + 8: 2, :]
                        st("v", lambda e, pbk=pbk, hsel=hsel: e.tensor_tensor(
                            hsel,
                            P[pbk].ap()[:].rearrange("p (a t) -> p a t", a=4),
                            tri[:].unsqueeze(1).broadcast_to((128, 4, 128)),
                            ALU.mult),
                           reads=[PB[pbk], buf("tri")], writes=[bAbw[par][w]], tag="mask")

            def emit_tail(c):
                tc = 128 * c
                par = c % 2
                for w in range(2):
                    def ofns(c=c, tc=tc, w=w, par=par):
                        fs = []
                        for j in range(4 * w, 4 * w + 4):
                            pbk = 4 + w
                            col = 128 * (j % 4)
                            out = P[pbk].ap()[:, col:col + 128]
                            lh_i = SBf[:, j, :]
                            rh_i = qsil[:, j, tc:tc + 128]
                            fs.append(lambda e, out=out, lh_i=lh_i, rh_i=rh_i: e.matmul(
                                out, lh_i, rh_i, start=True, stop=False))
                            for hh2 in range(2):
                                h = 2 * j + hh2
                                outh = P[pbk].ap()[64 * hh2:64 * (hh2 + 1), col:col + 128]
                                vsl = vv[:, c, 128 * j + 64 * hh2:128 * j + 64 * (hh2 + 1)]
                                asl = Abar[:, par, h, :]
                                fs.append(lambda e, outh=outh, vsl=vsl, asl=asl: e.matmul(
                                    outh, vsl, asl, start=False, stop=True))
                        return fs
                    st("t", ofns(), reads=[bSBf, bqs, bvv, bAbw[par][w]],
                       writes=[PB[4 + w]], tag="omm")
                def kvfns(c=c):
                    fs = []
                    for h in range(NH):
                        r0 = 64 * (h % 2)
                        col = 64 * (h // 2)
                        out = P[7].ap()[r0:r0 + 64, col:col + 64]
                        ksl = kT[:, c % 2, 64 * h:64 * (h + 1)]
                        vsl = vv[:, c, 64 * h:64 * (h + 1)]
                        fs.append(lambda e, out=out, ksl=ksl, vsl=vsl: e.matmul(
                            out, ksl, vsl, start=True, stop=True))
                    return fs
                st("t", kvfns(), reads=[bkTs[c % 2], bvv], writes=[PB[7]], tag="kv")
                for h2 in range(2):
                    sa = SA[64 * h2:64 * (h2 + 1), :, 64 * h2:64 * (h2 + 1)]
                    ps = P[7].ap()[64 * h2:64 * (h2 + 1), :].rearrange(
                        "p (j v) -> p j v", v=64)
                    st("v", lambda e, sa=sa, ps=ps: e.tensor_tensor(sa, sa, ps, ALU.add),
                       reads=[PB[7], bSA], writes=[bSA], tag="supd")
                st("v", lambda e, c=c: e.tensor_tensor(
                    SA[:], SA[:], ECb[:, :, c:c + 1].broadcast_to((128, NB, 128)),
                    ALU.mult),
                   reads=[bSA, bEC], writes=[bSA], tag="sscale")
                st("s", lambda e: e.copy(
                    SBf[:].rearrange("p a b -> p (a b)"),
                    SA[:].rearrange("p a b -> p (a b)")),
                   reads=[bSA], writes=[bSBf], tag="sbf")
                for q in range(2):
                    pbk = 4 + q
                    st("v", lambda e, pbk=pbk, q=q, tc=tc: e.tensor_tensor(
                        of[:, 4 * q:4 * (q + 1), tc:tc + 128],
                        P[pbk].ap()[:].rearrange("p (a t) -> p a t", a=4),
                        gsb[:, 4 * q:4 * (q + 1), tc:tc + 128], ALU.mult),
                       reads=[PB[pbk], bgs], writes=[bof] + AL_ofxnL, tag="oevac")
                st("s", lambda e, tc=tc: e.activation(
                    eq[:, :, tc:tc + 128], of[:, :, tc:tc + 128], AF.Square),
                   reads=[bof], writes=[beq] + AL_gseq, tag="square", no_selfwait=True)

            emit_ktr(0)
            for c in range(4):
                emit_scores(c)
                if c >= 1:
                    emit_tail(c - 1)
                if c < 3:
                    emit_ktr(c + 1)
            emit_tail(3)

            pend.append((m, t0, toff))
        while pend:
            emit_rms(*pend.pop(0))

    def router_chain(m, t0):
                g4 = (128, 4, 8, 4)

                def rv(fn, **kw):
                    st("v", fn, reads=[brt, buf("gbB"), buf("tri")], writes=[brt], **kw)
                rv(lambda e: e.tensor_tensor(
                    scb[:, m], scr[:, m],
                    gbB[:].unsqueeze(1).broadcast_to((128, 4, NE)),
                    ALU.add))
                sc4 = scb[:, m].rearrange("p a (g j) -> p a g j", j=4)
                rv(lambda e: e.tensor_tensor(rt1[:, m], sc4[:, :, :, 0], sc4[:, :, :, 1], ALU.max))
                rv(lambda e: e.tensor_tensor(rt2[:, m], sc4[:, :, :, 2], sc4[:, :, :, 3], ALU.max))
                rv(lambda e: e.tensor_tensor(rt3[:, m], sc4[:, :, :, 0], sc4[:, :, :, 1], ALU.min))
                rv(lambda e: e.tensor_tensor(gsum[:, m], sc4[:, :, :, 2], sc4[:, :, :, 3], ALU.min))
                rv(lambda e: e.tensor_tensor(gsum[:, m], gsum[:, m], rt3[:, m], ALU.max))
                rv(lambda e: e.tensor_tensor(rt3[:, m], rt1[:, m], rt2[:, m], ALU.min))
                rv(lambda e: e.tensor_tensor(rt3[:, m], rt3[:, m], gsum[:, m], ALU.max))
                rv(lambda e: e.tensor_tensor(gsum[:, m], rt1[:, m], rt2[:, m], ALU.max))
                rv(lambda e: e.tensor_tensor(gsum[:, m], gsum[:, m], rt3[:, m], ALU.add))
                for g in range(4):
                    rv(lambda e, g=g: e.max(out=gmax[:, m, g, :], in_=gsum[:, m, g, :]))
                rv(lambda e: e.memset(gmax[:, m, :, 4:8], -1e30))
                for g in range(4):
                    rv(lambda e, g=g: e.match_replace(
                        out=gzap[:, m, g, :], in_to_replace=gmax[:, m, g, :],
                        in_values=gsum[:, m, g, :], imm_value=-1e30))
                rv(lambda e: e.tensor_tensor(
                    gsum[:, m], gsum[:, m], gzap[:, m], ALU.is_gt))
                rv(lambda e: e.tensor_tensor(
                    selm[:, m].rearrange("p a (g j) -> p a g j", j=4),
                    scb[:, m].rearrange("p a (g j) -> p a g j", j=4),
                    gsum[:, m].unsqueeze(3).broadcast_to(g4),
                    ALU.mult))
                for g in range(4):
                    rv(lambda e, g=g: e.max(out=emax[:, m, g, :], in_=selm[:, m, g, :]))
                    rv(lambda e, g=g: e.match_replace(
                        out=ezap[:, m, g, :], in_to_replace=emax[:, m, g, :],
                        in_values=selm[:, m, g, :], imm_value=-1e30))
                rv(lambda e: e.tensor_tensor(
                    selm[:, m].rearrange("p a b -> p (a b)"),
                    selm[:, m].rearrange("p a b -> p (a b)"),
                    ezap[:, m].rearrange("p a b -> p (a b)"), ALU.is_gt))
                rv(lambda e: e.tensor_tensor(
                    tkw[:, m].rearrange("p a b -> p (a b)"),
                    scr[:, m].rearrange("p a b -> p (a b)"),
                    selm[:, m].rearrange("p a b -> p (a b)"), ALU.mult))
                rv(lambda e: e.tensor_reduce(tsum[:, m], tkw[:, m], mybir.AxisListType.X, ALU.add))
                rv(lambda e: e.tensor_scalar(tsum[:, m], tsum[:, m], 1e-20, None, ALU.add))
                rv(lambda e: e.reciprocal(tsum[:, m], tsum[:, m]))
                rv(lambda e: e.tensor_scalar(tsum[:, m], tsum[:, m], 2.5, None, ALU.mult))
                for g in range(4):
                    rv(lambda e, g=g: e.tensor_scalar(
                        combT[:, m, NE * g:NE * (g + 1)], tkw[:, m, g, :], tsum[:, m, g:g + 1],
                        None, ALU.mult))



    def router_comb(m, t0):
                    st("t", [lambda e, g=g: e.transpose(
                        P6b[0:NE, 128 * g:128 * (g + 1)], combT[:, m, NE * g:NE * (g + 1)],
                        ident[:]) for g in range(4)],
                       reads=[brt, buf("ident")], writes=[PB[6]], tag="combtr")
                    cFdst = cF[:, (t0 % SEG):(t0 % SEG) + MT]
                    st("v", lambda e, cFdst=cFdst: e.tensor_copy(cFdst, P6b[0:NE, 0:MT]),
                       reads=[PB[6]], writes=[bcFm[m]])

    def phase_b(seg):
        s0 = seg * SEG
        NT = SEG // MT
        st("g", lambda e: e.dma_start(
            out=WoS[:], in_=Wo_h.rearrange("(kc p) oc -> p kc oc", p=128)),
           writes=[bWo] + AL_Wo, dma="wos", tag="woload")
        for kc in range(NB):
            st("v", lambda e, kc=kc: e.tensor_scalar(
                WoS[:, kc, :], WoS[:, kc, :], nwF[:, kc:kc + 1], None, ALU.mult),
               reads=[bWo, buf("nwF")], writes=[bWo] + AL_Wo)
        for ti in range(NT):
            t0 = s0 + ti * MT
            st("y", lambda e, ti=ti, t0=t0: e.dma_start(
                out=outS[:, ti, :, :],
                in_=x_h[t0:t0 + MT, :].rearrange("(g p) c -> p g c", p=128)),
               writes=[boutS[ti]] + AL_outS, dma="oi%d" % ti, tag="outinit")
            st("y", lambda e, ti=ti, t0=t0: e.dma_start(
                out=onTs[ti][:],
                in_=on_h[:, :, t0:t0 + MT].transpose([1, 0, 2])),
               reads=[bon], writes=[bonT[ti]] + AL_onT[ti], dma="onld%d" % ti, tag="onload")

        jobs = [("sh", shgT_h[:, 0:512], shuT_h[:, 0:512], shdT_h[0:512, :]),
                ("sh", shgT_h[:, 512:1024], shuT_h[:, 512:1024], shdT_h[512:1024, :])]
        for ex in range(NE):
            jobs.append(("e%d" % ex, wgT_h[ex], wuT_h[ex], wdT_h[ex]))

        grps = [jobs[i:i + 2] for i in range(0, len(jobs), 2)]
        jseq = [0]

        for gi, grp in enumerate(grps):
            pars = []
            for js, (kind, gsrc, usrc, dsrc) in enumerate(grp):
                par = jseq[0] % 2
                jseq[0] += 1
                pars.append(par)
                st("g", lambda e, par=par, gsrc=gsrc: e.dma_start(
                    out=wgB[par][:],
                    in_=gsrc.rearrange("(kc p) i -> p kc i", p=128)),
                   writes=[bwg[par]] + AL_wg[par], dma="wg%d" % par, tag="wload")
                st("g", lambda e, par=par, usrc=usrc: e.dma_start(
                    out=wuB[par][:],
                    in_=usrc.rearrange("(kc p) i -> p kc i", p=128)),
                   writes=[bwu[par]] + AL_wu[par], dma="wu%d" % par, tag="wload")
                st("g", lambda e, par=par, dsrc=dsrc: e.dma_start(
                    out=wdB[par][:],
                    in_=dsrc.rearrange("(ic p) hh -> p ic hh", p=128)),
                   writes=[bwd[par]] + AL_wd[par], dma="wd%d" % par, tag="wload")
                is_sh = kind == "sh"
                eidx = None if is_sh else int(kind[1:])
                for ti in range(NT):
                    t0 = s0 + ti * MT
                    bslot = (js * NT + ti) % 2
                    if not is_sh:
                        st("t", lambda e, eidx=eidx, c0=(t0 % SEG): e.matmul(
                            P[4].ap()[:, :], sel32[:, eidx, :],
                            cF[:, c0:c0 + MT], start=True, stop=True),
                           reads=[buf("sel32"), bcFm[(t0 // MT) % NMT]], writes=[PB[4]], tag="bc")
                        st("s", lambda e, bslot=bslot: e.copy(
                            bcS[:, bslot, :], P[4].ap()[:, :]),
                           reads=[PB[4]], writes=[bbc[bslot]] + AL_bc)
                    for ic in range(4):
                        p2 = ic % 2
                        pga, pgb = 4 + 2 * p2, 5 + 2 * p2

                        def gfns(pga=pga, par=par, ic=ic, ti=ti):
                            out = P[pga].ap()[:, :]
                            return [lambda e, kc=kc, out=out: e.matmul(
                                out, wgB[par][:, kc, 128 * ic:128 * (ic + 1)],
                                xnH[:, kc, MT * ti:MT * (ti + 1)],
                                start=(kc == 0), stop=(kc == 7)) for kc in range(NB)]

                        def ufns(pgb=pgb, par=par, ic=ic, ti=ti):
                            out = P[pgb].ap()[:, :]
                            return [lambda e, kc=kc, out=out: e.matmul(
                                out, wuB[par][:, kc, 128 * ic:128 * (ic + 1)],
                                xnH[:, kc, MT * ti:MT * (ti + 1)],
                                start=(kc == 0), stop=(kc == 7)) for kc in range(NB)]
                        st("t", ufns(), reads=[bwu[par], bxnH], writes=[PB[pgb]], tag="umm")
                        st("t", gfns(), reads=[bwg[par], bxnH], writes=[PB[pga]], tag="gmm")
                        st("s", lambda e, pga=pga, p2=p2: e.activation(
                            sgS[:, p2, :], P[pga].ap()[:, :], AF.Silu),
                           reads=[PB[pga]], writes=[bsg[p2]] + AL_sg, tag="silu")
                        mdst = mEt[js][:, ti, ic, :]
                        st("v", lambda e, pgb=pgb, p2=p2, mdst=mdst: e.tensor_tensor(
                            mdst, P[pgb].ap()[:, :], sgS[:, p2, :], ALU.mult),
                           reads=[PB[pgb], bsg[p2]], writes=[bmE[js][ti]] + AL_mE[js], tag="m1")
                        if not is_sh:
                            st("v", lambda e, mdst=mdst, bslot=bslot: e.tensor_tensor(
                                mdst, mdst, bcS[:, bslot, :], ALU.mult),
                               reads=[bbc[bslot], bmE[js][ti]], writes=[bmE[js][ti]] + AL_mE[js], tag="m2")
            if gi == 0:
                for m in range(NMT):
                    router_chain(m, s0 + m * MT)
            for ti in range(NT):
                t0 = s0 + ti * MT
                for hh in range(2):
                    for tg in range(4):
                        def dfns(ti=ti, hh=hh, tg=tg, gi=gi):
                            fs = []
                            first = True
                            if gi == 0:
                                for kc in range(NB):
                                    fs.append(lambda e, kc=kc, ti=ti, hh=hh, tg=tg,
                                              first=first and kc == 0: e.matmul(
                                        P[tg].ap()[:, :],
                                        onTs[ti][:, kc, 128 * tg:128 * (tg + 1)],
                                        WoS[:, kc, 512 * hh:512 * (hh + 1)],
                                        start=(kc == 0), stop=False))
                                first = False
                            for js in range(len(grp)):
                                par = pars[js]
                                for ic in range(4):
                                    fst = first and ic == 0 and js == 0
                                    lst = js == len(grp) - 1 and ic == 3
                                    fs.append(lambda e, js=js, par=par, ic=ic, ti=ti,
                                              hh=hh, tg=tg, fst=fst, lst=lst: e.matmul(
                                        P[tg].ap()[:, :],
                                        mEt[js][:, ti, ic, 128 * tg:128 * (tg + 1)],
                                        wdB[par][:, ic, 512 * hh:512 * (hh + 1)],
                                        start=fst, stop=lst))
                            return fs
                        reads = [bmE[0][ti], bmE[len(grp) - 1][ti],
                                 bwd[pars[0]], bwd[pars[-1]]]
                        if gi == 0:
                            reads += [bonT[ti], bWo]
                        st("t", dfns(), reads=reads, writes=[PB[tg]], tag="dmm")
                    for tg in range(4):
                        st("v", lambda e, ti=ti, hh=hh, tg=tg: e.tensor_tensor(
                            outS[:, ti, tg, 512 * hh:512 * (hh + 1)],
                            outS[:, ti, tg, 512 * hh:512 * (hh + 1)],
                            P[tg].ap()[:, :], ALU.add),
                           reads=[PB[tg], boutS[ti]], writes=[boutS[ti]] + AL_outS, tag="devac")
            if gi == 0:
                for m in range(NMT):
                    router_comb(m, s0 + m * MT)
        dump("B_outS", outS[:, 0, :, :], [boutS[0]])
        dump("B_mE", mE0[:, :, :, :], [bmE[0][0], bmE[0][1]])
        for ti in range(NT):
            t0 = s0 + ti * MT
            st("y", lambda e, ti=ti, t0=t0: e.dma_start(
                out=out_h[t0:t0 + MT, :].rearrange("(g p) c -> p g c", p=128),
                in_=outS[:, ti, :, :]),
               reads=[boutS[ti]], writes=[bout], dma="outst", tag="store")

    for seg in range(NSEG):
        phase_a(seg)
        phase_b(seg)
    mk.finalize()
    mk.emit()
    nc._mk_sched = mk
    return nc



_CACHE = {}


def _layout(inputs, T):
    C = np.ascontiguousarray
    return {
        "Wq": inputs["L0_attn_Wq"], "Wk": inputs["L0_attn_Wk"],
        "Wv": inputs["L0_attn_Wv"], "Wo": inputs["L0_attn_Wo"],
        "delta": inputs["L0_attn_delta"], "G1": inputs["L0_attn_G1"],
        "G2": inputs["L0_attn_G2"], "norm_w": inputs["L0_attn_norm_w"],
        "gwT": C(np.asarray(inputs["L0_ffn_gate_w"]).T),
        "gate_bias": inputs["L0_ffn_gate_bias"],
        "wgT": C(np.asarray(inputs["L0_ffn_w_gate"]).transpose(0, 2, 1)),
        "wuT": C(np.asarray(inputs["L0_ffn_w_up"]).transpose(0, 2, 1)),
        "wdT": C(np.asarray(inputs["L0_ffn_w_down"]).transpose(0, 2, 1)),
        "shgT": C(np.asarray(inputs["L0_ffn_sh_gate"]).T),
        "shuT": C(np.asarray(inputs["L0_ffn_sh_up"]).T),
        "shdT": C(np.asarray(inputs["L0_ffn_sh_down"]).T),
    }


def kernel(**inputs):
    x = np.asarray(inputs["L0_attn_x"])
    B, T, _ = x.shape
    nc = build(T=T, SEG=min(1024, T))
    w = _layout(inputs, T)
    in_maps = [dict(w, x=np.ascontiguousarray(x[b])) for b in range(B)]
    res = run_bass_kernel_spmd(nc, in_maps, core_ids=list(range(B)))
    return np.stack([np.asarray(r["out"]) for r in res.results], axis=0)


# version 19
# speedup vs baseline: 1.0627x; 1.0370x over previous
import numpy as np
import concourse.bass as bass
import concourse.mybir as mybir
from concourse.bass_utils import run_bass_kernel_spmd

F32 = mybir.dt.float32
BF16 = mybir.dt.bfloat16
I32 = mybir.dt.int32
U8 = mybir.dt.uint8
AF = mybir.ActivationFunctionType
ALU = mybir.AluOpType

E = 1024
NH = 16
HD = 64
NB = 8
MT = 512
CH = 128
NE = 32
EI = 512
EPS = 1e-5



class Buf:
    __slots__ = ("w", "r", "name")

    def __init__(self, name=""):
        self.w = []
        self.r = []
        self.name = name


class MK:
    ENGS = ("t", "v", "s", "g", "y")

    SERIAL = False

    def __init__(self, nc):
        self.nc = nc
        self.eng = {"t": nc.tensor, "v": nc.vector, "s": nc.scalar,
                    "g": nc.gpsimd, "y": nc.sync}
        self.sem = {k: nc.alloc_semaphore(name=f"ms_{k}") for k in self.ENGS}
        self.cnt = {k: 0 for k in self.ENGS}
        self.steps = {k: [] for k in self.ENGS}
        self.dsem = {}

    def dma(self, name):
        if name not in self.dsem:
            self.dsem[name] = [self.nc.alloc_semaphore(name=f"dma_{name}"), 0]
        return self.dsem[name]

    def step(self, ek, fns, reads=(), writes=(), dma=None, tag="", no_selfwait=False):
        if not isinstance(fns, (list, tuple)):
            fns = [fns]
        deps = {}
        selfw = [0]

        def add(tok):
            sem, val = tok
            if sem is self.sem[ek]:
                if val >= self.cnt[ek] - 1:
                    selfw[0] = max(selfw[0], val)
                return
            k = sem.num
            if k not in deps or deps[k][1] < val:
                deps[k] = (sem, val)

        for b in reads:
            for t in b.w:
                add(t)
        for b in writes:
            for t in b.w:
                add(t)
            for t in b.r:
                add(t)
        if self.SERIAL:
            for k2 in self.ENGS:
                if self.cnt[k2]:
                    add((self.sem[k2], self.cnt[k2]))
            for (h, c) in self.dsem.values():
                if c:
                    add((h, c))
        if selfw[0] and not no_selfwait:
            deps[self.sem[ek].num] = (self.sem[ek], selfw[0])
        if dma is None:
            self.cnt[ek] += 1
            tok = (self.sem[ek], self.cnt[ek])
            inc = (self.sem[ek], 1)
        else:
            ds = self.dma(dma)
            ds[1] += 16
            tok = (ds[0], ds[1])
            inc = (ds[0], 16)
        self.steps[ek].append((list(deps.values()), fns, inc, tag))
        for b in reads:
            b.r.append(tok)
        for b in writes:
            b.w = [tok]
            b.r = []
        return tok

    def barrier(self, skip=()):
        toks = [(self.sem[k], self.cnt[k]) for k in self.ENGS]
        toks += [(h, c) for (h, c) in self.dsem.values()]
        for k in self.ENGS:
            if k in skip:
                continue
            deps = [(s, v) for (s, v) in toks if s is not self.sem[k]]
            self.cnt[k] += 1
            self.steps[k].append(
                (deps, [lambda e: e.nop()], (self.sem[k], 1), "barrier"))

    def finalize(self):
        self.barrier()

    def emit(self):
        with self.nc.Block() as block:
            for k in self.ENGS:
                def runner(e, k=k):
                    seen = {}
                    for deps, fns, inc, tag in self.steps[k]:
                        for sem, val in deps:
                            if seen.get(sem.num, -1) < val:
                                e.wait_ge(sem, val)
                                seen[sem.num] = val
                        inst = None
                        for f in fns:
                            inst = f(e)
                        inst.then_inc(inc[0], inc[1])
                getattr(block, {"t": "tensor", "v": "vector", "s": "scalar",
                                "g": "gpsimd", "y": "sync"}[k])(runner)



def build(T=4096, SEG=1024, dbg=()):
    assert T % SEG == 0 and SEG % MT == 0
    NSEG = T // SEG
    NMT = SEG // MT

    nc = bass.Bass("TRN2", target_bir_lowering=False)
    mk = MK(nc)

    dt_in = {}

    def din(name, shape, dtype=F32):
        h = nc.dram_tensor(name, list(shape), dtype, kind="ExternalInput").ap()
        dt_in[name] = h
        return h

    x_h = din("x", (T, E))
    Wq_h = din("Wq", (E, E)); Wk_h = din("Wk", (E, E)); Wv_h = din("Wv", (E, E))
    Wo_h = din("Wo", (E, E))
    dlt_h = din("delta", (E,))
    G1_h = din("G1", (E, HD)); G2_h = din("G2", (HD, E))
    nw_h = din("norm_w", (E,))
    gwT_h = din("gwT", (E, NE))
    gb_h = din("gate_bias", (NE,))
    wgT_h = din("wgT", (NE, E, EI)); wuT_h = din("wuT", (NE, E, EI))
    wdT_h = din("wdT", (NE, EI, E))
    shgT_h = din("shgT", (E, E)); shuT_h = din("shuT", (E, E))
    shdT_h = din("shdT", (E, E))

    out_h = nc.dram_tensor("out", [T, E], F32, kind="ExternalOutput").ap()
    on_h = nc.dram_tensor("on_scr", [NB, 128, T], BF16, kind="Internal").ap()

    P = [nc.alloc_psum_tensor(f"bank{i}", [128, 512], F32) for i in range(8)]
    PB = [Buf(f"P{i}") for i in range(8)]
    P6b = P[6].ap().bitcast(BF16)
    P7b = P[7].ap().bitcast(BF16)

    def sb(name, shape, dtype):
        return nc.alloc_sbuf_tensor(name, list(shape), dtype)

    xnH = sb("xnH", (128, NB, SEG), BF16)
    cF = sb("cF", (NE, SEG), BF16)
    SA = sb("SA", (128, NB, 128), F32)
    SBf = sb("SBf", (128, NB, 128), BF16)
    kT = sb("kT", (128, 2, E), BF16)
    Abar = sb("Abar", (128, 2, NH, 128), BF16)
    ECb = sb("ECb", (128, NB, 4), F32)
    tri = sb("tri", (128, 128), F32)
    ident = sb("ident", (128, 128), BF16)
    ones1 = sb("ones1", (1, 128), BF16)
    ones1f = sb("ones1f", (1, 128), F32)
    sel32 = sb("sel32", (NE, NE, 128), BF16)
    id32 = sb("id32", (NE, NE), BF16)
    id16 = sb("id16", (16, 16), BF16)
    hi2 = sb("hi2", (128, 2), BF16)
    sel16 = sb("sel16", (128, NB, 16), BF16)
    selB = sb("selB", (16, NB, 128), BF16)
    posd = sb("posd", (128, NB), F32)
    negd = sb("negd", (128, NB), F32)
    nwF = sb("nwF", (128, NB), F32)
    gwF = sb("gwF", (128, NB, NE), F32)
    gwH = sb("gwH", (128, NB, NE), BF16)
    gwL = sb("gwL", (128, NB, NE), BF16)
    gb1 = sb("gb1", (1, NE), F32)
    gbB = sb("gbB", (128, NE), F32)
    epsH = sb("epsH", (16, 1), F32)
    epsLN = sb("epsLN", (128, 1), F32)
    G1b = sb("G1b", (128, NB, HD), BF16)
    G2b = sb("G2b", (HD, E), BF16)
    rsH = sb("rsH", (16, 512), BF16)
    rsL = sb("rsL", (16, 512), F32)
    bnst = sb("bnst", (128, 12), F32)
    bnag = sb("bnag", (128, 2), F32)
    lnt = sb("lnt", (128, 1), F32)
    scr = sb("scr", (128, 2, 4, NE), F32)
    scb = sb("scb", (128, 2, 4, NE), F32)
    gsum = sb("gsum", (128, 2, 4, 8), F32)
    gmax = sb("gmax", (128, 2, 4, 8), F32)
    gzap = sb("gzap", (128, 2, 4, 8), F32)
    emax = sb("emax", (128, 2, 4, 8), F32)
    ezap = sb("ezap", (128, 2, 4, NE), F32)
    selm = sb("selm", (128, 2, 4, NE), F32)
    tkw = sb("tkw", (128, 2, 4, NE), F32)
    tsum = sb("tsum", (128, 2, 4), F32)
    rt1 = sb("rt1", (128, 2, 4, 8), F32)
    rt2 = sb("rt2", (128, 2, 4, 8), F32)
    rt3 = sb("rt3", (128, 2, 4, 8), F32)
    combT = sb("combT", (128, 2, 128), BF16)

    ARENA = 147456
    arena = nc.alloc_sbuf_tensor("arena", [128, ARENA], U8)
    arena_addr = nc.lookup_mloc(arena).addr

    def place(name, shape, dtype, off):
        per = int(np.prod(shape[1:])) * mybir.dt.size(dtype)
        assert off + per <= ARENA, (name, off, per)
        h = nc.virtual_sbuf_tensor(name, list(shape), dtype)
        nc._tensor(h.name, h.shape, h.dtype, type="SB")
        mloc = nc.lookup_mloc(h)
        mloc.allocated = True
        mloc.addr = arena_addr + off
        return h

    xA = place("xA", (128, 2, 1024), F32, 0)
    WkB = place("WkB", (128, NB, E), BF16, 8192)
    tmA = place("tmA", (128, 1024), BF16, 24576)
    tmB = place("tmB", (128, 1024), BF16, 26624)
    xg1 = place("xg1", (HD, 512), BF16, 28672)
    WvB = place("WvB", (128, NB, E), BF16, 32768)
    WqB = place("WqB", (128, NB, E), BF16, 49152)
    fg = place("fg", (128, NB, 512), F32, 65536)
    qsil = place("qsil", (128, NB, 512), BF16, 81920)
    kk = place("kk", (128, NB, 512), BF16, 90112)
    vv = place("vv", (128, 4, 1024), BF16, 98304)
    gsb = place("gsb", (128, NB, 512), BF16, 106496)
    eq = place("eq", (128, NB, 512), BF16, 114688)
    of = place("of", (128, NB, 512), BF16, 122880)
    xnL = place("xnL", (128, NB, 512), BF16, 131072)
    ek = place("ek", (128, NB, 512), BF16, 131072)
    onT0 = place("onT0", (128, NB, 512), BF16, 0)
    WoS = place("WoS", (128, NB, E), BF16, 8192)
    onT1 = place("onT1", (128, NB, 512), BF16, 24576)
    wgB1 = place("wgB1", (128, NB, 512), BF16, 32768)
    wuB1 = place("wuB1", (128, NB, 512), BF16, 40960)
    mE0 = place("mE0", (128, 2, 4, 512), BF16, 49152)
    wgB0 = place("wgB0", (128, NB, 512), BF16, 57344)
    mE1 = place("mE1", (128, 2, 4, 512), BF16, 65536)
    wuB0 = place("wuB0", (128, NB, 512), BF16, 73728)
    wdB0 = place("wdB0", (128, 4, 1024), BF16, 81920)
    wdB1 = place("wdB1", (128, 4, 1024), BF16, 90112)
    sgS = place("sgS", (128, 2, 512), BF16, 98304)
    bcS = place("bcS", (128, 2, 512), BF16, 100352)
    outS = place("outS", (128, 2, 4, 1024), F32, 114688)
    iotaI = place("iotaI", (128, 128), I32, 139264)
    wgB = [wgB0, wgB1]; wuB = [wuB0, wuB1]; wdB = [wdB0, wdB1]
    onTs = [onT0, onT1]; mEt = [mE0, mE1]

    rem = nc.sbuf_bytes_remaining
    assert rem >= 0, f"SBUF over budget by {-rem} bytes"

    _dumped = set()

    def dump(tag, src_ap, read_bufs):
        if tag not in dbg or tag in _dumped:
            return
        _dumped.add(tag)
        h = nc.dram_tensor("dbg_" + tag, list(src_ap.shape), src_ap.dtype,
                           kind="ExternalOutput").ap()
        idx = tuple(slice(None) for _ in src_ap.shape)
        mk.step("y", lambda e: e.dma_start(out=h[idx], in_=src_ap),
                reads=list(read_bufs), dma="dbg_%s" % tag, tag="dbg")

    B_ = {}

    def buf(name):
        if name not in B_:
            B_[name] = Buf(name)
        return B_[name]

    st = mk.step
    st("g", lambda e: e.iota(iotaI[:], pattern=[[-1, 128]], base=0,
                             channel_multiplier=1), writes=[buf("iota")])
    st("v", lambda e: e.tensor_scalar(tri[:], iotaI[:], 0.0, None, ALU.is_le),
       reads=[buf("iota")], writes=[buf("tri")])
    st("v", lambda e: e.tensor_scalar(ident[:], iotaI[:], 0.0, None, ALU.is_equal),
       reads=[buf("iota")], writes=[buf("ident")])
    st("v", lambda e: e.memset(ones1[:], 1.0), writes=[buf("ones1")])
    st("v", lambda e: e.memset(ones1f[:], 1.0), writes=[buf("ones1f")])
    st("v", lambda e: e.memset(epsH[:], EPS), writes=[buf("epsH")])
    st("v", lambda e: e.memset(epsLN[:], EPS), writes=[buf("epsLN")])
    st("v", lambda e: e.memset(SA[:], 0.0), writes=[buf("SA")])
    st("v", lambda e: e.memset(SBf[:], 0.0), writes=[buf("SBf")])
    st("v", lambda e: e.memset(sel16[:], 0.0), writes=[buf("sel16")])
    st("v", lambda e: e.tensor_scalar(id32[:], iotaI[0:NE, 0:NE], 0.0, None, ALU.is_equal),
       reads=[buf("iota")], writes=[buf("id32")])
    st("v", lambda e: e.tensor_copy(
        sel32[:], id32[:].unsqueeze(2).broadcast_to((NE, NE, 128))),
       reads=[buf("id32")], writes=[buf("sel32")])
    st("v", lambda e: e.tensor_scalar(id16[:], iotaI[0:16, 0:16], 0.0, None, ALU.is_equal),
       reads=[buf("iota")], writes=[buf("id16")])
    st("v", lambda e: e.tensor_copy(
        selB[:].rearrange("k b (h r) -> k b h r", h=2),
        id16[:].rearrange("k (b h) -> k b h", h=2).unsqueeze(3).broadcast_to((16, NB, 2, 64))),
       reads=[buf("id16")], writes=[buf("selB")])
    st("v", lambda e: e.tensor_scalar(hi2[:, 0:1], iotaI[:, 0:1], 64.0, None, ALU.is_lt),
       reads=[buf("iota")], writes=[buf("hi2")])
    st("v", lambda e: e.tensor_scalar(hi2[:, 1:2], iotaI[:, 0:1], 63.0, None, ALU.is_gt),
       reads=[buf("iota")], writes=[buf("hi2")])
    sel16f = sel16[:].rearrange("p a b -> p (a b)")
    for h2 in range(2):
        import dataclasses as _dc
        dst = _dc.replace(sel16f, offset=sel16f.offset + h2,
                          ap=[list(sel16f.ap)[0], [18, 8]])
        st("v", lambda e, dst=dst, h2=h2: e.tensor_copy(
            dst, hi2[:, h2:h2 + 1].broadcast_to((128, 8))),
           reads=[buf("hi2")], writes=[buf("sel16")])
    st("y", lambda e: e.dma_start(out=posd[:], in_=dlt_h.rearrange("(b p) -> p b", p=128), allow_slow_non_contiguous=True),
       writes=[buf("posd")], dma="ldposd")
    st("v", lambda e: e.tensor_scalar(negd[:], posd[:], -1.0, None, ALU.mult),
       reads=[buf("posd")], writes=[buf("negd")])
    st("y", lambda e: e.dma_start(out=nwF[:], in_=nw_h.rearrange("(b p) -> p b", p=128), allow_slow_non_contiguous=True),
       writes=[buf("nwF")], dma="ldnw")
    st("y", lambda e: e.dma_start(out=gwF[:], in_=gwT_h.rearrange("(b p) ne -> p b ne", p=128)),
       writes=[buf("gwF")], dma="ldgw")
    st("v", lambda e: e.tensor_copy(gwH[:], gwF[:]), reads=[buf("gwF")], writes=[buf("gwH")])
    st("v", lambda e: e.scalar_tensor_tensor(gwL[:], gwF[:], 0.0, gwH[:],
                                             ALU.bypass, ALU.subtract),
       reads=[buf("gwF"), buf("gwH")], writes=[buf("gwL")])
    st("y", lambda e: e.dma_start(out=gb1[:], in_=gb_h[None, :]),
       writes=[buf("gb1")], dma="ldgb")
    st("t", lambda e: e.matmul(P[0].ap()[:, 0:NE], ones1f[0:1, :], gb1[0:1, :],
                               start=True, stop=True),
       reads=[buf("ones1f"), buf("gb1")], writes=[PB[0]])
    st("v", lambda e: e.tensor_copy(gbB[:], P[0].ap()[:, 0:NE]),
       reads=[PB[0]], writes=[buf("gbB")])
    st("g", lambda e: e.dma_start(out=G1b[:], in_=G1_h.rearrange("(b p) d -> p b d", p=128)),
       writes=[buf("G1b")], dma="ldg1")
    st("g", lambda e: e.dma_start(out=G2b[:], in_=G2_h[:, :]),
       writes=[buf("G2b")], dma="ldg2")

    bxA = [buf("xA0"), buf("xA1")]
    bfg = buf("fg"); bqs = buf("qsil"); bkk = buf("kk"); bvv = buf("vv")
    bgs = buf("gsb"); beq = buf("eq"); bof = buf("of")
    bxnL = buf("xnL"); bek = bxnL; btmA = buf("tmA"); btmB = buf("tmB"); bxg1 = buf("xg1")
    bWq = buf("WqB"); bWk = buf("WkB"); bWv = buf("WvB")
    bxnH = buf("xnH"); bcFm = [buf("cF0"), buf("cF1")]; bSA = buf("SA"); bSBf = buf("SBf")
    bkTs = [buf("kT0"), buf("kT1")]
    bAbw = [[buf("Abar00"), buf("Abar01")], [buf("Abar10"), buf("Abar11")]]
    bEC = buf("ECb")
    brt = buf("rt")
    brsH = buf("rsH"); brsL = buf("rsL"); bbn = buf("bn")
    bon = buf("on_scr")
    bmE = [[buf("mE%d%d" % (a, b)) for b in range(2)] for a in range(2)]
    bwg = [buf("wg0"), buf("wg1")]
    bwu = [buf("wu0"), buf("wu1")]; bwd = [buf("wd0"), buf("wd1")]
    bWo = buf("WoS"); bonT = [buf("onT0"), buf("onT1")]
    boutS = [buf("outS0"), buf("outS1")]
    bsg = [buf("sg0"), buf("sg1")]; bbc = [buf("bc0"), buf("bc1")]
    bout = buf("out_h")

    AL_onT = [[bxA[0], bxA[1]], [btmA, btmB, bxg1]]
    AL_wg = [[bWq], [bWv]]
    AL_wu = [[bfg], [bWv]]
    AL_wd = [[bqs], [bkk]]
    AL_Wo = [bWk]
    AL_mE = [[bWq], [bfg]]
    AL_outS = [beq, bof, bxnL, buf("iota")]
    AL_sg = [bvv]
    AL_bc = [bvv]
    AL_xA = [bonT[0]]
    AL_fg = bmE[1] + [bwu[0]]
    AL_qsil = [bwd[0]]
    AL_kk = [bwd[1]]
    AL_vv = [bsg[0], bsg[1], bbc[0], bbc[1]]
    AL_gseq = [boutS[0], boutS[1]]
    AL_ofxnL = [boutS[0], boutS[1]]
    AL_tmA = [bonT[1]]
    AL_tmB = [bonT[1]]
    AL_WqB = bmE[0] + [bwg[0]]
    AL_WkB = [bWo]
    AL_WvB = [bwg[1], bwu[1]]

    def phase_a(seg):
        s0 = seg * SEG
        pend = []
        def emit_rms(m, t0, toff):
                st("t", [lambda e, blk=blk: e.matmul(
                    P[0].ap()[0:16, :], sel16[:, blk, :], eq[:, blk, :],
                    start=(blk == 0), stop=(blk == 7)) for blk in range(NB)],
                   reads=[beq, buf("sel16")], writes=[PB[0]], tag="rmssum")
                st("s", lambda e: e.activation(rsL[:], P[0].ap()[0:16, :], AF.Ln,
                                               bias=epsH[:], scale=1.0 / HD),
                   reads=[PB[0], buf("epsH")], writes=[brsL], tag="rms_rs")
                st("s", lambda e: e.activation(rsH[:], rsL[:], AF.Exp, scale=-0.5),
                   reads=[brsL], writes=[brsH], tag="rms_rs")
                for blk in range(NB):
                    pbk = blk % 2
                    st("t", lambda e, blk=blk, pbk=pbk: e.matmul(
                        P[pbk].ap()[:, :], selB[:, blk, :], rsH[:],
                        start=True, stop=True),
                       reads=[brsH, buf("selB")], writes=[PB[pbk]], tag="rsbc")
                    st("v", lambda e, blk=blk, pbk=pbk: e.tensor_tensor(
                        of[:, blk, :], of[:, blk, :], P[pbk].ap()[:, :], ALU.mult),
                       reads=[PB[pbk], bof], writes=[bof] + AL_ofxnL, tag="onmul")
                dump("A_of", of[:, :, :], [bof])
                dump("A_SA", SA[:, :, :], [bSA])
                dump("A_Abar", Abar[:, 1, :, :], [bAbw[1][0], bAbw[1][1]])
                ondst = on_h[:, :, t0:t0 + MT].transpose([1, 0, 2])
                st("y", lambda e, ondst=ondst: e.dma_start(out=ondst, in_=of[:]),
                   reads=[bof], writes=[bon], dma="onst", tag="onstore")

        for nm, W_h, bW, dst in (("wqs", Wq_h, bWq, WqB), ("wks", Wk_h, bWk, WkB),
                                 ("wvs", Wv_h, bWv, WvB)):
            st("g", lambda e, W_h=W_h, dst=dst: e.dma_start(
                out=dst[:], in_=W_h.rearrange("(kc p) oc -> p kc oc", p=128)),
               writes=[bW] + (AL_WqB if dst is WqB else AL_WkB if dst is WkB else AL_WvB), dma=nm, tag="wqkv")

        for m in range(NMT):
            t0 = s0 + m * MT
            toff = m * MT
            for g in range(4):
                slot = g % 2
                xa = xA[:, slot, :]
                xsrc = x_h[t0 + 128 * g: t0 + 128 * (g + 1), :]
                if not (seg > 0 and m == 0 and g < 2):
                    st("y", lambda e, xa=xa, xsrc=xsrc: e.dma_start(out=xa, in_=xsrc),
                       writes=[bxA[slot]] + AL_xA, dma="xin%d" % slot, tag="xload")
                st("v", [lambda e, xa=xa: e.bn_stats(bnst[:, 0:6], xa[:, 0:512]),
                         lambda e, xa=xa: e.bn_stats(bnst[:, 6:12], xa[:, 512:1024])],
                   reads=[bxA[slot]], writes=[bbn])
                st("v", lambda e: e.bn_aggr(bnag[:], bnst[:]),
                   reads=[bbn], writes=[bbn])
                st("s", lambda e: e.activation(lnt[:], bnag[:, 1:2], AF.Ln, bias=epsLN[:]),
                   reads=[bbn, buf("epsLN")], writes=[buf("lnt")])
                st("s", lambda e: e.activation(lnt[:], lnt[:], AF.Exp, scale=-0.5),
                   reads=[buf("lnt")], writes=[buf("lnt")])
                st("v", lambda e, xa=xa: e.tensor_scalar(
                    xa, xa, bnag[:, 0:1], lnt[:], ALU.subtract, ALU.mult),
                   reads=[bbn, buf("lnt")], writes=[bxA[slot]] + AL_xA)
                st("s", lambda e, xa=xa: e.copy(tmA[:], xa),
                   reads=[bxA[slot]], writes=[btmA] + AL_tmA)
                st("v", lambda e, xa=xa: e.scalar_tensor_tensor(
                    tmB[:], xa, 0.0, tmA[:], ALU.bypass, ALU.subtract),
                   reads=[bxA[slot], btmA], writes=[btmB] + AL_tmB)
                st("t", [lambda e, b=b: e.transpose(
                    P6b[:, 128 * b:128 * (b + 1)], tmA[:, 128 * b:128 * (b + 1)], ident[:])
                    for b in range(NB)],
                   reads=[btmA, buf("ident")], writes=[PB[6]], tag="xntr")
                st("t", [lambda e, b=b: e.transpose(
                    P7b[:, 128 * b:128 * (b + 1)], tmB[:, 128 * b:128 * (b + 1)], ident[:])
                    for b in range(NB)],
                   reads=[btmB, buf("ident")], writes=[PB[7]], tag="xntr")
                xnHdst = xnH[:, :, toff + 128 * g: toff + 128 * (g + 1)]
                st("s", lambda e, xnHdst=xnHdst: e.copy(
                    xnHdst, P6b[:].rearrange("p (b t) -> p b t", b=NB)),
                   reads=[PB[6]], writes=[bxnH])
                st("s", lambda e, g=g: e.copy(
                    xnL[:, :, 128 * g: 128 * (g + 1)],
                    P7b[:].rearrange("p (b t) -> p b t", b=NB)),
                   reads=[PB[7]], writes=[bxnL] + AL_ofxnL)

            xnHm = xnH[:, :, toff:toff + MT]
            while pend and pend[0][0] < m:
                emit_rms(*pend.pop(0))
            def rfns():
                fs = []
                for g in range(4):
                    out = P[2].ap()[:, NE * g:NE * (g + 1)]
                    for i, (lh, rh) in enumerate(
                            ((xnHm, gwH), (xnHm, gwL), (xnL, gwH))):
                        for kc in range(NB):
                            lhs = lh[:, kc, 128 * g:128 * (g + 1)]
                            fs.append(lambda e, out=out, lhs=lhs, rh=rh, kc=kc,
                                      first=(i == 0 and kc == 0),
                                      last=(i == 2 and kc == 7): e.matmul(
                                out, lhs, rh[:, kc, :], start=first, stop=last))
                return fs
            st("t", rfns(), reads=[bxnH, bxnL, buf("gwH"), buf("gwL")],
               writes=[PB[2]], tag="router_mm")
            scrdst = scr[:, m, :, :].rearrange("p a b -> p (a b)")
            st("s", lambda e, scrdst=scrdst: e.activation(
                scrdst, P[2].ap()[:, 0:128], AF.Sigmoid),
               reads=[PB[2]], writes=[brt], tag="sig")
            xnHm = xnH[:, :, toff:toff + MT]

            def proj_fm(dstW, bW, pb, blk, extra_reads):
                def fns(xnHm=xnHm):
                    out = P[pb].ap()[:, :]
                    return [lambda e, kc=kc, out=out, xnHm=xnHm: e.matmul(
                        out, dstW[:, kc, 128 * blk:128 * (blk + 1)],
                        xnHm[:, kc, :], start=(kc == 0), stop=(kc == 7))
                        for kc in range(NB)]
                return st("t", fns(), reads=[bW, bxnH] + extra_reads,
                          writes=[PB[pb]], tag="proj")

            for blk in range(NB):
                pb = blk % 4
                proj_fm(WkB, bWk, pb, blk, [])
                st("s", lambda e, pb=pb, blk=blk: e.activation(
                    kk[:, blk, :], P[pb].ap()[:, :], AF.Sigmoid,
                    bias=negd[:, blk:blk + 1], scale=-1.0),
                   reads=[PB[pb], buf("negd")], writes=[bkk] + AL_kk, tag="sig")
                st("s", lambda e, pb=pb, blk=blk: e.activation(
                    fg[:, blk, :], P[pb].ap()[:, :], AF.Sigmoid,
                    bias=posd[:, blk:blk + 1], scale=1.0),
                   reads=[PB[pb], buf("posd")], writes=[bfg] + AL_fg, tag="sig")
            st("t", [lambda e, kc=kc, xnHm=xnHm: e.matmul(
                P[0].ap()[0:HD, :], G1b[:, kc, :], xnHm[:, kc, :],
                start=(kc == 0), stop=(kc == 7)) for kc in range(NB)],
               reads=[buf("G1b"), bxnH], writes=[PB[0]], tag="xg1")
            st("v", lambda e: e.tensor_copy(xg1[:], P[0].ap()[0:HD, :]),
               reads=[PB[0]], writes=[bxg1, bonT[1]])
            st("s", lambda e: e.activation(
                fg[:].rearrange("p a b -> p (a b)"),
                fg[:].rearrange("p a b -> p (a b)"), AF.Ln),
               reads=[bfg], writes=[bfg] + AL_fg, tag="ln_g")
            for blk in range(NB):
                for c in range(4):
                    sl = fg[:, blk, 128 * c:128 * (c + 1)]
                    st("v", lambda e, sl=sl: e.tensor_tensor_scan(
                        sl, sl, sl, 0.0, ALU.add, ALU.bypass),
                       reads=[bfg], writes=[bfg] + AL_fg, no_selfwait=True)
            st("s", lambda e: e.activation(
                ECb[:].rearrange("p a b -> p (a b)"),
                fg[:, :, 127::128].rearrange("p a b -> p (a b)"), AF.Exp),
               reads=[bfg], writes=[bEC], tag="exp")
            st("s", lambda e: e.activation(
                eq[:].rearrange("p a b -> p (a b)"),
                fg[:].rearrange("p a b -> p (a b)"), AF.Exp, scale=1.0),
               reads=[bfg], writes=[beq] + AL_gseq, tag="exp")
            st("s", lambda e: e.activation(
                ek[:].rearrange("p a b -> p (a b)"),
                fg[:].rearrange("p a b -> p (a b)"), AF.Exp, scale=-1.0),
               reads=[bfg], writes=[bek] + AL_ofxnL, tag="exp")
            for g in range(4):
                for oh in range(2):
                    pb = g * 2 + oh

                    def fns(g=g, oh=oh, pb=pb, xnHm=xnHm):
                        out = P[pb].ap()[:, :]
                        return [lambda e, kc=kc, g=g, oh=oh, out=out, xnHm=xnHm:
                                e.matmul(
                            out, xnHm[:, kc, 128 * g:128 * (g + 1)],
                            WvB[:, kc, 512 * oh:512 * (oh + 1)],
                            start=(kc == 0), stop=(kc == 7)) for kc in range(NB)]
                    st("t", fns(), reads=[bWv, bxnH], writes=[PB[pb]], tag="vproj")
                    st("v", lambda e, pb=pb, g=g, oh=oh: e.tensor_copy(
                        vv[:, g, 512 * oh:512 * (oh + 1)], P[pb].ap()[:, :]),
                       reads=[PB[pb]], writes=[bvv] + AL_vv)
            st("v", lambda e: e.tensor_tensor(
                kk[:].rearrange("p a b -> p (a b)"),
                kk[:].rearrange("p a b -> p (a b)"),
                ek[:].rearrange("p a b -> p (a b)"), ALU.mult),
               reads=[bkk, bek], writes=[bkk] + AL_kk)
            for blk in range(NB):
                pb = blk % 4
                proj_fm(WqB, bWq, pb, blk, [])
                st("s", lambda e, pb=pb, blk=blk: e.activation(
                    qsil[:, blk, :], P[pb].ap()[:, :], AF.Silu),
                   reads=[PB[pb]], writes=[bqs] + AL_qsil, tag="qsilu")
            st("v", lambda e: e.tensor_tensor(
                qsil[:].rearrange("p a b -> p (a b)"),
                qsil[:].rearrange("p a b -> p (a b)"),
                eq[:].rearrange("p a b -> p (a b)"), ALU.mult),
               reads=[bqs, beq], writes=[bqs] + AL_qsil)
            for blk in range(NB):
                pb = 4 + blk % 2
                st("t", lambda e, blk=blk, pb=pb: e.matmul(
                    P[pb].ap()[:, :], G2b[:, 128 * blk:128 * (blk + 1)], xg1[:],
                    start=True, stop=True),
                   reads=[buf("G2b"), bxg1], writes=[PB[pb]], tag="gate")
                st("s", lambda e, pb=pb, blk=blk: e.activation(
                    gsb[:, blk, :], P[pb].ap()[:, :], AF.Sigmoid),
                   reads=[PB[pb]], writes=[bgs], tag="sig")

            dump("A_xn", xnH[:, :, toff:toff + MT], [bxnH])
            dump("A_bnag", bnag[:, :], [bbn])
            dump("A_lnt", lnt[:, :], [buf("lnt")])
            dump("A_xa", xA[:, :, :], [bxA[0], bxA[1]])
            dump("A_xnL", xnL[:, :, :], [bxnL])
            dump("A_B", fg[:, :, :], [bfg])
            dump("A_q", qsil[:, :, :], [bqs])
            dump("A_k", kk[:, :, :], [bkk])
            dump("A_v", vv[:, :, :], [bvv])
            dump("A_gs", gsb[:, :, :], [bgs])
            dump("A_eq", eq[:, :, :], [beq])
            dump("A_EC", ECb[:, :, :], [bEC])
            dump("A_cF", cF[:, 0:MT], [bcFm[0]])
            def emit_ktr(c):
                tc = 128 * c
                st("t", [lambda e, b=b, src=kk[:, b, tc:tc + 128]: e.transpose(
                    P6b[:, 128 * b:128 * (b + 1)], src, ident[:])
                    for b in range(NB)],
                   reads=[bkk, buf("ident")], writes=[PB[6]], tag="ktr")
                st("s", lambda e, sl=kT[:, c % 2, :]: e.copy(sl, P6b[:]),
                   reads=[PB[6]], writes=[bkTs[c % 2]], tag="ktev")

            def emit_scores(c):
                tc = 128 * c
                par = c % 2
                for w in range(2):
                    pb0 = 2 - 2 * w
                    def sfns(w=w, pb0=pb0, tc=tc):
                        fs = []
                        for j in range(4 * w, 4 * w + 4):
                            col = 128 * (j % 4)
                            for hh2 in range(2):
                                pbk = pb0 + hh2
                                lhs = kk[64 * hh2:64 * (hh2 + 1), j, tc:tc + 128]
                                rhs = qsil[64 * hh2:64 * (hh2 + 1), j, tc:tc + 128]
                                fs.append(lambda e, pbk=pbk, col=col, lhs=lhs, rhs=rhs:
                                          e.matmul(P[pbk].ap()[:, col:col + 128],
                                                   lhs, rhs, start=True, stop=True))
                        return fs
                    st("t", sfns(), reads=[bkk, bqs], writes=[PB[pb0], PB[pb0 + 1]],
                       tag="score")
                    for hh2 in range(2):
                        pbk = pb0 + hh2
                        hsel = Abar[:, par, 8 * w + hh2: 8 * w + 8: 2, :]
                        st("v", lambda e, pbk=pbk, hsel=hsel: e.tensor_tensor(
                            hsel,
                            P[pbk].ap()[:].rearrange("p (a t) -> p a t", a=4),
                            tri[:].unsqueeze(1).broadcast_to((128, 4, 128)),
                            ALU.mult),
                           reads=[PB[pbk], buf("tri")], writes=[bAbw[par][w]], tag="mask")

            def emit_tail(c):
                tc = 128 * c
                par = c % 2
                for w in range(2):
                    def ofns(c=c, tc=tc, w=w, par=par):
                        fs = []
                        for j in range(4 * w, 4 * w + 4):
                            pbk = 4 + w
                            col = 128 * (j % 4)
                            out = P[pbk].ap()[:, col:col + 128]
                            lh_i = SBf[:, j, :]
                            rh_i = qsil[:, j, tc:tc + 128]
                            fs.append(lambda e, out=out, lh_i=lh_i, rh_i=rh_i: e.matmul(
                                out, lh_i, rh_i, start=True, stop=False,
                                skip_group_check=True))
                            for hh2 in range(2):
                                h = 2 * j + hh2
                                outh = P[pbk].ap()[64 * hh2:64 * (hh2 + 1), col:col + 128]
                                vsl = vv[:, c, 128 * j + 64 * hh2:128 * j + 64 * (hh2 + 1)]
                                asl = Abar[:, par, h, :]
                                fs.append(lambda e, outh=outh, vsl=vsl, asl=asl: e.matmul(
                                    outh, vsl, asl, start=False, stop=True,
                                    skip_group_check=True))
                        return fs
                    st("t", ofns(), reads=[bSBf, bqs, bvv, bAbw[par][w]],
                       writes=[PB[4 + w]], tag="omm")
                def kvfns(c=c):
                    fs = []
                    for h in range(NH):
                        r0 = 64 * (h % 2)
                        col = 64 * (h // 2)
                        out = P[7].ap()[r0:r0 + 64, col:col + 64]
                        ksl = kT[:, c % 2, 64 * h:64 * (h + 1)]
                        vsl = vv[:, c, 64 * h:64 * (h + 1)]
                        fs.append(lambda e, out=out, ksl=ksl, vsl=vsl: e.matmul(
                            out, ksl, vsl, start=True, stop=True))
                    return fs
                st("t", kvfns(), reads=[bkTs[c % 2], bvv], writes=[PB[7]], tag="kv")
                for h2 in range(2):
                    sa = SA[64 * h2:64 * (h2 + 1), :, 64 * h2:64 * (h2 + 1)]
                    ps = P[7].ap()[64 * h2:64 * (h2 + 1), :].rearrange(
                        "p (j v) -> p j v", v=64)
                    st("v", lambda e, sa=sa, ps=ps: e.tensor_tensor(sa, sa, ps, ALU.add),
                       reads=[PB[7], bSA], writes=[bSA], tag="supd")
                st("v", lambda e, c=c: e.tensor_tensor(
                    SA[:], SA[:], ECb[:, :, c:c + 1].broadcast_to((128, NB, 128)),
                    ALU.mult),
                   reads=[bSA, bEC], writes=[bSA], tag="sscale")
                st("s", lambda e: e.copy(
                    SBf[:].rearrange("p a b -> p (a b)"),
                    SA[:].rearrange("p a b -> p (a b)")),
                   reads=[bSA], writes=[bSBf], tag="sbf")
                for q in range(2):
                    pbk = 4 + q
                    st("v", lambda e, pbk=pbk, q=q, tc=tc: e.tensor_tensor(
                        of[:, 4 * q:4 * (q + 1), tc:tc + 128],
                        P[pbk].ap()[:].rearrange("p (a t) -> p a t", a=4),
                        gsb[:, 4 * q:4 * (q + 1), tc:tc + 128], ALU.mult),
                       reads=[PB[pbk], bgs], writes=[bof] + AL_ofxnL, tag="oevac")
                st("s", lambda e, tc=tc: e.activation(
                    eq[:, :, tc:tc + 128], of[:, :, tc:tc + 128], AF.Square),
                   reads=[bof], writes=[beq] + AL_gseq, tag="square", no_selfwait=True)

            emit_ktr(0)
            for c in range(4):
                emit_scores(c)
                if c >= 1:
                    emit_tail(c - 1)
                if c < 3:
                    emit_ktr(c + 1)
            emit_tail(3)

            pend.append((m, t0, toff))
        while pend:
            emit_rms(*pend.pop(0))

    def router_chain(m, t0):
                g4 = (128, 4, 8, 4)

                def rv(fn, **kw):
                    st("v", fn, reads=[brt, buf("gbB"), buf("tri")], writes=[brt], **kw)
                rv(lambda e: e.tensor_tensor(
                    scb[:, m], scr[:, m],
                    gbB[:].unsqueeze(1).broadcast_to((128, 4, NE)),
                    ALU.add))
                sc4 = scb[:, m].rearrange("p a (g j) -> p a g j", j=4)
                rv(lambda e: e.tensor_tensor(rt1[:, m], sc4[:, :, :, 0], sc4[:, :, :, 1], ALU.max))
                rv(lambda e: e.tensor_tensor(rt2[:, m], sc4[:, :, :, 2], sc4[:, :, :, 3], ALU.max))
                rv(lambda e: e.tensor_tensor(rt3[:, m], sc4[:, :, :, 0], sc4[:, :, :, 1], ALU.min))
                rv(lambda e: e.tensor_tensor(gsum[:, m], sc4[:, :, :, 2], sc4[:, :, :, 3], ALU.min))
                rv(lambda e: e.tensor_tensor(gsum[:, m], gsum[:, m], rt3[:, m], ALU.max))
                rv(lambda e: e.tensor_tensor(rt3[:, m], rt1[:, m], rt2[:, m], ALU.min))
                rv(lambda e: e.tensor_tensor(rt3[:, m], rt3[:, m], gsum[:, m], ALU.max))
                rv(lambda e: e.tensor_tensor(gsum[:, m], rt1[:, m], rt2[:, m], ALU.max))
                rv(lambda e: e.tensor_tensor(gsum[:, m], gsum[:, m], rt3[:, m], ALU.add))
                for g in range(4):
                    rv(lambda e, g=g: e.max(out=gmax[:, m, g, :], in_=gsum[:, m, g, :]))
                rv(lambda e: e.memset(gmax[:, m, :, 4:8], -1e30))
                for g in range(4):
                    rv(lambda e, g=g: e.match_replace(
                        out=gzap[:, m, g, :], in_to_replace=gmax[:, m, g, :],
                        in_values=gsum[:, m, g, :], imm_value=-1e30))
                rv(lambda e: e.tensor_tensor(
                    gsum[:, m], gsum[:, m], gzap[:, m], ALU.is_gt))
                rv(lambda e: e.tensor_tensor(
                    selm[:, m].rearrange("p a (g j) -> p a g j", j=4),
                    scb[:, m].rearrange("p a (g j) -> p a g j", j=4),
                    gsum[:, m].unsqueeze(3).broadcast_to(g4),
                    ALU.mult))
                for g in range(4):
                    rv(lambda e, g=g: e.max(out=emax[:, m, g, :], in_=selm[:, m, g, :]))
                    rv(lambda e, g=g: e.match_replace(
                        out=ezap[:, m, g, :], in_to_replace=emax[:, m, g, :],
                        in_values=selm[:, m, g, :], imm_value=-1e30))
                rv(lambda e: e.tensor_tensor(
                    selm[:, m].rearrange("p a b -> p (a b)"),
                    selm[:, m].rearrange("p a b -> p (a b)"),
                    ezap[:, m].rearrange("p a b -> p (a b)"), ALU.is_gt))
                rv(lambda e: e.tensor_tensor(
                    tkw[:, m].rearrange("p a b -> p (a b)"),
                    scr[:, m].rearrange("p a b -> p (a b)"),
                    selm[:, m].rearrange("p a b -> p (a b)"), ALU.mult))
                rv(lambda e: e.tensor_reduce(tsum[:, m], tkw[:, m], mybir.AxisListType.X, ALU.add))
                rv(lambda e: e.tensor_scalar(tsum[:, m], tsum[:, m], 1e-20, None, ALU.add))
                rv(lambda e: e.reciprocal(tsum[:, m], tsum[:, m]))
                rv(lambda e: e.tensor_scalar(tsum[:, m], tsum[:, m], 2.5, None, ALU.mult))
                for g in range(4):
                    rv(lambda e, g=g: e.tensor_scalar(
                        combT[:, m, NE * g:NE * (g + 1)], tkw[:, m, g, :], tsum[:, m, g:g + 1],
                        None, ALU.mult))



    def router_comb(m, t0):
                    st("t", [lambda e, g=g: e.transpose(
                        P6b[0:NE, 128 * g:128 * (g + 1)], combT[:, m, NE * g:NE * (g + 1)],
                        ident[:]) for g in range(4)],
                       reads=[brt, buf("ident")], writes=[PB[6]], tag="combtr")
                    cFdst = cF[:, (t0 % SEG):(t0 % SEG) + MT]
                    st("v", lambda e, cFdst=cFdst: e.tensor_copy(cFdst, P6b[0:NE, 0:MT]),
                       reads=[PB[6]], writes=[bcFm[m]])

    def phase_b(seg):
        s0 = seg * SEG
        NT = SEG // MT
        st("g", lambda e: e.dma_start(
            out=WoS[:], in_=Wo_h.rearrange("(kc p) oc -> p kc oc", p=128)),
           writes=[bWo] + AL_Wo, dma="wos", tag="woload")
        for kc in range(NB):
            st("v", lambda e, kc=kc: e.tensor_scalar(
                WoS[:, kc, :], WoS[:, kc, :], nwF[:, kc:kc + 1], None, ALU.mult),
               reads=[bWo, buf("nwF")], writes=[bWo] + AL_Wo)
        for ti in range(NT):
            t0 = s0 + ti * MT
            st("y", lambda e, ti=ti, t0=t0: e.dma_start(
                out=outS[:, ti, :, :],
                in_=x_h[t0:t0 + MT, :].rearrange("(g p) c -> p g c", p=128)),
               writes=[boutS[ti]] + AL_outS, dma="oi%d" % ti, tag="outinit")
            st("y", lambda e, ti=ti, t0=t0: e.dma_start(
                out=onTs[ti][:],
                in_=on_h[:, :, t0:t0 + MT].transpose([1, 0, 2])),
               reads=[bon], writes=[bonT[ti]] + AL_onT[ti], dma="onld%d" % ti, tag="onload")

        jobs = [("sh", shgT_h[:, 0:512], shuT_h[:, 0:512], shdT_h[0:512, :]),
                ("sh", shgT_h[:, 512:1024], shuT_h[:, 512:1024], shdT_h[512:1024, :])]
        for ex in range(NE):
            jobs.append(("e%d" % ex, wgT_h[ex], wuT_h[ex], wdT_h[ex]))

        grps = [jobs[i:i + 2] for i in range(0, len(jobs), 2)]
        jseq = [0]

        for gi, grp in enumerate(grps):
            pars = []
            for js, (kind, gsrc, usrc, dsrc) in enumerate(grp):
                par = jseq[0] % 2
                jseq[0] += 1
                pars.append(par)
                st("g", lambda e, par=par, gsrc=gsrc: e.dma_start(
                    out=wgB[par][:],
                    in_=gsrc.rearrange("(kc p) i -> p kc i", p=128)),
                   writes=[bwg[par]] + AL_wg[par], dma="wg%d" % par, tag="wload")
                st("g", lambda e, par=par, usrc=usrc: e.dma_start(
                    out=wuB[par][:],
                    in_=usrc.rearrange("(kc p) i -> p kc i", p=128)),
                   writes=[bwu[par]] + AL_wu[par], dma="wu%d" % par, tag="wload")
                st("g", lambda e, par=par, dsrc=dsrc: e.dma_start(
                    out=wdB[par][:],
                    in_=dsrc.rearrange("(ic p) hh -> p ic hh", p=128)),
                   writes=[bwd[par]] + AL_wd[par], dma="wd%d" % par, tag="wload")
                is_sh = kind == "sh"
                eidx = None if is_sh else int(kind[1:])
                for ti in range(NT):
                    t0 = s0 + ti * MT
                    bslot = (js * NT + ti) % 2
                    if not is_sh:
                        st("t", lambda e, eidx=eidx, c0=(t0 % SEG): e.matmul(
                            P[4].ap()[:, :], sel32[:, eidx, :],
                            cF[:, c0:c0 + MT], start=True, stop=True),
                           reads=[buf("sel32"), bcFm[(t0 // MT) % NMT]], writes=[PB[4]], tag="bc")
                        st("s", lambda e, bslot=bslot: e.copy(
                            bcS[:, bslot, :], P[4].ap()[:, :]),
                           reads=[PB[4]], writes=[bbc[bslot]] + AL_bc)
                    for ic in range(4):
                        p2 = ic % 2
                        pga, pgb = 4 + 2 * p2, 5 + 2 * p2

                        def gfns(pga=pga, par=par, ic=ic, ti=ti):
                            out = P[pga].ap()[:, :]
                            return [lambda e, kc=kc, out=out: e.matmul(
                                out, wgB[par][:, kc, 128 * ic:128 * (ic + 1)],
                                xnH[:, kc, MT * ti:MT * (ti + 1)],
                                start=(kc == 0), stop=(kc == 7)) for kc in range(NB)]

                        def ufns(pgb=pgb, par=par, ic=ic, ti=ti):
                            out = P[pgb].ap()[:, :]
                            return [lambda e, kc=kc, out=out: e.matmul(
                                out, wuB[par][:, kc, 128 * ic:128 * (ic + 1)],
                                xnH[:, kc, MT * ti:MT * (ti + 1)],
                                start=(kc == 0), stop=(kc == 7)) for kc in range(NB)]
                        st("t", ufns(), reads=[bwu[par], bxnH], writes=[PB[pgb]], tag="umm")
                        st("t", gfns(), reads=[bwg[par], bxnH], writes=[PB[pga]], tag="gmm")
                        st("s", lambda e, pga=pga, p2=p2: e.activation(
                            sgS[:, p2, :], P[pga].ap()[:, :], AF.Silu),
                           reads=[PB[pga]], writes=[bsg[p2]] + AL_sg, tag="silu")
                        mdst = mEt[js][:, ti, ic, :]
                        st("v", lambda e, pgb=pgb, p2=p2, mdst=mdst: e.tensor_tensor(
                            mdst, P[pgb].ap()[:, :], sgS[:, p2, :], ALU.mult),
                           reads=[PB[pgb], bsg[p2]], writes=[bmE[js][ti]] + AL_mE[js], tag="m1")
                        if not is_sh:
                            st("v", lambda e, mdst=mdst, bslot=bslot: e.tensor_tensor(
                                mdst, mdst, bcS[:, bslot, :], ALU.mult),
                               reads=[bbc[bslot], bmE[js][ti]], writes=[bmE[js][ti]] + AL_mE[js], tag="m2")
            if gi == 0:
                for m in range(NMT):
                    router_chain(m, s0 + m * MT)
            for ti in range(NT):
                t0 = s0 + ti * MT
                for hh in range(2):
                    for tg in range(4):
                        def dfns(ti=ti, hh=hh, tg=tg, gi=gi):
                            fs = []
                            first = True
                            if gi == 0:
                                for kc in range(NB):
                                    fs.append(lambda e, kc=kc, ti=ti, hh=hh, tg=tg,
                                              first=first and kc == 0: e.matmul(
                                        P[tg].ap()[:, :],
                                        onTs[ti][:, kc, 128 * tg:128 * (tg + 1)],
                                        WoS[:, kc, 512 * hh:512 * (hh + 1)],
                                        start=(kc == 0), stop=False))
                                first = False
                            for js in range(len(grp)):
                                par = pars[js]
                                for ic in range(4):
                                    fst = first and ic == 0 and js == 0
                                    lst = js == len(grp) - 1 and ic == 3
                                    fs.append(lambda e, js=js, par=par, ic=ic, ti=ti,
                                              hh=hh, tg=tg, fst=fst, lst=lst: e.matmul(
                                        P[tg].ap()[:, :],
                                        mEt[js][:, ti, ic, 128 * tg:128 * (tg + 1)],
                                        wdB[par][:, ic, 512 * hh:512 * (hh + 1)],
                                        start=fst, stop=lst))
                            return fs
                        reads = [bmE[0][ti], bmE[len(grp) - 1][ti],
                                 bwd[pars[0]], bwd[pars[-1]]]
                        if gi == 0:
                            reads += [bonT[ti], bWo]
                        st("t", dfns(), reads=reads, writes=[PB[tg]], tag="dmm")
                    for tg in range(4):
                        st("v", lambda e, ti=ti, hh=hh, tg=tg: e.tensor_tensor(
                            outS[:, ti, tg, 512 * hh:512 * (hh + 1)],
                            outS[:, ti, tg, 512 * hh:512 * (hh + 1)],
                            P[tg].ap()[:, :], ALU.add),
                           reads=[PB[tg], boutS[ti]], writes=[boutS[ti]] + AL_outS, tag="devac")
            if gi == 0:
                for m in range(NMT):
                    router_comb(m, s0 + m * MT)
                if seg + 1 < NSEG:
                    for g2 in (0, 1):
                        t0n = (seg + 1) * SEG + 128 * g2
                        xa = xA[:, g2, :]
                        xsrc = x_h[t0n:t0n + 128, :]
                        st("y", lambda e, xa=xa, xsrc=xsrc: e.dma_start(
                            out=xa, in_=xsrc),
                           writes=[bxA[g2]] + AL_xA, dma="xin%d" % g2, tag="xpre")
        dump("B_outS", outS[:, 0, :, :], [boutS[0]])
        dump("B_mE", mE0[:, :, :, :], [bmE[0][0], bmE[0][1]])
        for ti in range(NT):
            t0 = s0 + ti * MT
            st("y", lambda e, ti=ti, t0=t0: e.dma_start(
                out=out_h[t0:t0 + MT, :].rearrange("(g p) c -> p g c", p=128),
                in_=outS[:, ti, :, :]),
               reads=[boutS[ti]], writes=[bout], dma="outst", tag="store")

    for seg in range(NSEG):
        phase_a(seg)
        phase_b(seg)
    mk.finalize()
    mk.emit()
    nc._mk_sched = mk
    return nc



_CACHE = {}


def _layout(inputs, T):
    C = np.ascontiguousarray
    return {
        "Wq": inputs["L0_attn_Wq"], "Wk": inputs["L0_attn_Wk"],
        "Wv": inputs["L0_attn_Wv"], "Wo": inputs["L0_attn_Wo"],
        "delta": inputs["L0_attn_delta"], "G1": inputs["L0_attn_G1"],
        "G2": inputs["L0_attn_G2"], "norm_w": inputs["L0_attn_norm_w"],
        "gwT": C(np.asarray(inputs["L0_ffn_gate_w"]).T),
        "gate_bias": inputs["L0_ffn_gate_bias"],
        "wgT": C(np.asarray(inputs["L0_ffn_w_gate"]).transpose(0, 2, 1)),
        "wuT": C(np.asarray(inputs["L0_ffn_w_up"]).transpose(0, 2, 1)),
        "wdT": C(np.asarray(inputs["L0_ffn_w_down"]).transpose(0, 2, 1)),
        "shgT": C(np.asarray(inputs["L0_ffn_sh_gate"]).T),
        "shuT": C(np.asarray(inputs["L0_ffn_sh_up"]).T),
        "shdT": C(np.asarray(inputs["L0_ffn_sh_down"]).T),
    }


def kernel(**inputs):
    x = np.asarray(inputs["L0_attn_x"])
    B, T, _ = x.shape
    nc = build(T=T, SEG=min(1024, T))
    w = _layout(inputs, T)
    in_maps = [dict(w, x=np.ascontiguousarray(x[b])) for b in range(B)]
    res = run_bass_kernel_spmd(nc, in_maps, core_ids=list(range(B)))
    return np.stack([np.asarray(r["out"]) for r in res.results], axis=0)
